# version 02 (final)
# baseline (speedup 1.0000x reference)
import numpy as np
import ml_dtypes
import concourse.bass as bass
import concourse.mybir as mybir
from concourse.bass_utils import run_bass_kernel_spmd

F32 = mybir.dt.float32
BF16 = mybir.dt.bfloat16
I32 = mybir.dt.int32
AF = mybir.ActivationFunctionType
ALU = mybir.AluOpType
AX = mybir.AxisListType.X

L = 4096
D = 2048
HK, HV, DK, DV = 16, 32, 128, 128
NT = 8
TT = 512
C = 128
NCH = 32
SEGC = 16
EPS_LN = 1e-5
EPS = 1e-6
NEG = -1e30


def host_constants():
    bf = lambda a: np.asarray(a, np.float32).astype(ml_dtypes.bfloat16)
    c = {}
    c["c_id_bf"] = bf(np.eye(128))
    c["c_id_f32"] = np.eye(128, dtype=np.float32)
    c["c_ones_1x128_bf"] = bf(np.ones((1, 128)))
    c["c_ones_1x128_f32"] = np.ones((1, 128), dtype=np.float32)
    c["c_ones_col_bf"] = bf(np.ones((128, 1)))
    j = np.arange(128)[:, None]
    i = np.arange(128)[None, :]
    c["c_maskstrict"] = np.where(j < i, 0.0, NEG).astype(np.float32)
    c["c_U128"] = np.triu(np.ones((128, 128), np.float32))
    sel = np.zeros((4, 512), np.float32)
    for h in range(4):
        sel[h, h * 128:(h + 1) * 128] = 1.0
    c["c_sel4"] = sel
    c["c_negsel4"] = -sel
    c["c_maskstrict4"] = np.tile(c["c_maskstrict"], (1, 4))
    c["c_id01_4"] = np.tile(np.eye(128, dtype=np.float32), (1, 4))
    for c0 in (0, 1):
        for half in (0, 1):
            m = np.zeros((128, 512), np.float32)
            if c0 == 0:
                m[:, 0:256] = NEG
            qpos = half * 128 + np.arange(128)[:, None]
            jj = np.arange(256)[None, :]
            m[:, 256:512] = np.where(jj <= qpos, m[:, 256:512], NEG)
            c[f"c_wmask{c0}{half}"] = m
    inv = (1.0 / 10000.0 ** (np.arange(0, DK, 2, dtype=np.float32) / DK))
    t = np.arange(L, dtype=np.float32)
    fr = np.outer(t, inv).astype(np.float32)
    c["c_cos"] = np.ascontiguousarray(np.cos(fr).T)
    c["c_sin"] = np.ascontiguousarray(np.sin(fr).T)
    c["c_col_epsln"] = np.full((128, 1), EPS_LN, np.float32)
    c["c_col_eps6"] = np.full((128, 1), EPS, np.float32)
    c["c_row1_eps6"] = np.full((1, 1), EPS, np.float32)
    c["c_row1_lnqs"] = np.full((1, 1), -0.5 * np.log(128.0), np.float32)
    return c


CONST_SPECS = [
    ("c_id_bf", [128, 128], BF16),
    ("c_id_f32", [128, 128], F32),
    ("c_ones_1x128_bf", [1, 128], BF16),
    ("c_ones_1x128_f32", [1, 128], F32),
    ("c_ones_col_bf", [128, 1], BF16),
    ("c_maskstrict", [128, 128], F32),
    ("c_U128", [128, 128], F32),
    ("c_sel4", [4, 512], F32),
    ("c_negsel4", [4, 512], F32),
    ("c_maskstrict4", [128, 512], F32),
    ("c_id01_4", [128, 512], F32),
    ("c_wmask00", [128, 512], F32),
    ("c_wmask01", [128, 512], F32),
    ("c_wmask10", [128, 512], F32),
    ("c_wmask11", [128, 512], F32),
    ("c_cos", [64, L], F32),
    ("c_sin", [64, L], F32),
    ("c_col_epsln", [128, 1], F32),
    ("c_col_eps6", [128, 1], F32),
    ("c_row1_eps6", [1, 1], F32),
    ("c_row1_lnqs", [1, 1], F32),
]


def permute_rope_weight(w):
    perm = np.concatenate([np.arange(0, DK, 2), np.arange(1, DK, 2)])
    fullperm = np.concatenate([h * DK + perm for h in range(HK)])
    return np.ascontiguousarray(np.asarray(w)[fullperm])


class Sem:
    def __init__(self, nc, name):
        self.h = nc.alloc_semaphore(name)
        self.n = 0


class Prog:
    def __init__(self, nc, dev):
        self.nc = nc
        self.dev = dev
        self.d = {}
        self.sb = {}

    def dram(self, name, shape, dtype, kind="Internal"):
        if self.dev and kind == "Internal":
            kind = "ExternalOutput"
        ap = self.nc.dram_tensor(name, shape, dtype, kind=kind).ap()
        self.d[name] = ap
        return ap


def emit_ln_transpose(p, x_ap, xn_ap, tag, add_ap=None, xout_ap=None):
    nc = p.nc
    sLd = [Sem(nc, f"{tag}ld0"), Sem(nc, f"{tag}ld1")]
    sAd = [Sem(nc, f"{tag}ad0"), Sem(nc, f"{tag}ad1")]
    sC = Sem(nc, f"{tag}c")
    sF = Sem(nc, f"{tag}f")
    sFs = Sem(nc, f"{tag}fs")
    sNg = Sem(nc, f"{tag}ng")
    sRs = Sem(nc, f"{tag}rs")
    sXn = Sem(nc, f"{tag}xn")
    sAs = Sem(nc, f"{tag}as")
    sTr = Sem(nc, f"{tag}tr")
    sEv = Sem(nc, f"{tag}ev")
    sSt = [Sem(nc, f"{tag}st0"), Sem(nc, f"{tag}st1")]
    sXo = [Sem(nc, f"{tag}xo0"), Sem(nc, f"{tag}xo1")]
    do_ln = xn_ap is not None
    do_add = add_ap is not None

    import contextlib
    with contextlib.ExitStack() as _st:
        en = _st.enter_context
        xt = en(nc.sbuf_tensor(f"{tag}_xt", [128, 2 * D], F32))
        if do_add:
            at = en(nc.sbuf_tensor(f"{tag}_at", [128, 2 * D], F32))
            xs = en(nc.sbuf_tensor(f"{tag}_xs", [128, 2 * D], F32))
        else:
            xs = xt
        cols = en(nc.sbuf_tensor(f"{tag}_cols", [128, 8], F32))
        red_t = en(nc.sbuf_tensor(f"{tag}_red_t", [128, 1], F32))
        ss_t = en(nc.sbuf_tensor(f"{tag}_ss_t", [128, 1], F32))
        epsln = en(nc.sbuf_tensor(f"{tag}_epsln", [128, 1], F32))
        if do_ln:
            sq_scr = en(nc.sbuf_tensor(f"{tag}_sq_scr", [128, D], BF16))
            xnt = en(nc.sbuf_tensor(f"{tag}_xnt", [128, 2 * D], BF16))
            stage = en(nc.sbuf_tensor(f"{tag}_stage", [128, 2 * 16 * TT], BF16))
            ptr = [en(nc.psum_tensor(f"{tag}_ptr{i}", [128, 128], BF16)) for i in range(2)]
            idb = en(nc.sbuf_tensor(f"{tag}_idb", [128, 128], BF16))

        with nc.Block() as blk:
            @blk.sync
            def _(sync):
                if do_ln:
                    sync.dma_start(idb[:], p.d["c_id_bf"][:]).then_inc(sC.h, 16)
                    sC.n += 16
                sync.dma_start(epsln[:], p.d["c_col_epsln"][:]).then_inc(sC.h, 16)
                sC.n += 16
                for st in range(32):
                    if st >= 2:
                        sync.wait_ge(sAs.h if do_add else sXn.h, st - 1)
                    sync.dma_start(
                        xt[:, (st % 2) * D:(st % 2 + 1) * D],
                        x_ap[st * 128:(st + 1) * 128, :]
                    ).then_inc(sLd[st % 2].h, 16)
                    sLd[st % 2].n += 16

            @blk.vector
            def _(v):
                for st in range(32):
                    buf = st % 2
                    v.wait_ge(sLd[buf].h, 16 * (st // 2 + 1))
                    if st == 0:
                        v.wait_ge(sC.h, sC.n)
                    xv = xs[:, buf * D:(buf + 1) * D]
                    if do_add:
                        v.wait_ge(sAd[buf].h, 16 * (st // 2 + 1))
                        if st >= 2:
                            v.wait_ge(sXo[buf].h, 16 * (st // 2))
                            if do_ln:
                                v.wait_ge(sRs.h, st - 1)
                                v.wait_ge(sXn.h, st - 1)
                        v.tensor_add(xv, xt[:, buf * D:(buf + 1) * D],
                                     at[:, buf * D:(buf + 1) * D]
                                     ).then_inc(sAs.h, 1)
                        sAs.n += 1
                        v.wait_ge(sAs.h, sAs.n)
                    if not do_ln:
                        continue
                    if not do_add:
                        if st >= 2:
                            v.wait_ge(sRs.h, st - 1)
                    if st >= 1:
                        v.wait_ge(sNg.h, st)
                    v.reduce_sum(red_t[:], xv, axis=AX).then_inc(sF.h, 1)
                    sF.n += 1
                    v.wait_ge(sF.h, sF.n)
                    v.tensor_scalar_mul(cols[:, 1 + buf:2 + buf], red_t[:],
                                        -1.0 / D).then_inc(sNg.h, 1)
                    sNg.n += 1
                    if st >= 1:
                        pbuf = (st - 1) % 2
                        v.wait_ge(sRs.h, st)
                        if st >= 3:
                            v.wait_ge(sTr.h, 16 * (st - 2))
                        v.tensor_scalar(
                            xnt[:, pbuf * D:(pbuf + 1) * D],
                            xs[:, pbuf * D:(pbuf + 1) * D],
                            cols[:, 1 + pbuf:2 + pbuf],
                            cols[:, 5 + pbuf:6 + pbuf],
                            ALU.add, ALU.mult).then_inc(sXn.h, 1)
                        sXn.n += 1
                if do_ln:
                    v.wait_ge(sRs.h, 32)
                    v.wait_ge(sTr.h, 16 * 30)
                    v.tensor_scalar(
                        xnt[:, D:2 * D], xs[:, D:2 * D],
                        cols[:, 2:3], cols[:, 6:7], ALU.add, ALU.mult
                    ).then_inc(sXn.h, 1)
                    sXn.n += 1

            if do_ln:
                @blk.scalar
                def _(s):
                    def evac(st):
                        tb = st // 4
                        sub = st % 4
                        sbuf = tb % 2
                        if st >= 8 and sub == 0:
                            s.wait_ge(sSt[tb % 2].h, 16 * (tb // 2))
                        for kc in range(16):
                            n = st * 16 + kc
                            bank = n % 2
                            s.wait_ge(sTr.h, n + 1)
                            s.copy(
                                stage[:, sbuf * 16 * TT + kc * TT + sub * 128:
                                      sbuf * 16 * TT + kc * TT + sub * 128 + 128],
                                ptr[bank][:]).then_inc(sEv.h, 1)
                            sEv.n += 1

                    for st in range(32):
                        buf = st % 2
                        if do_add:
                            s.wait_ge(sAs.h, st + 1)
                        s.wait_ge(sNg.h, st + 1)
                        xv = xs[:, buf * D:(buf + 1) * D]
                        ins = s.activation(sq_scr[:], xv, AF.Square,
                                           bias=cols[:, 1 + buf:2 + buf],
                                           accum_out=ss_t[:])
                        fence(s, sFs, ins)
                        ins = s.activation(cols[:, 4:5], ss_t[:], AF.Ln,
                                           bias=epsln[:], scale=1.0 / D)
                        fence(s, sFs, ins)
                        s.activation(cols[:, 5 + buf:6 + buf], cols[:, 4:5],
                                     AF.Exp, scale=-0.5).then_inc(sRs.h, 1)
                        sRs.n += 1
                        if st >= 1:
                            evac(st - 1)
                    evac(31)

                @blk.tensor
                def _(t):
                    for st in range(32):
                        buf = st % 2
                        t.wait_ge(sXn.h, st + 1)
                        for kc in range(16):
                            n = st * 16 + kc
                            bank = n % 2
                            if n >= 2:
                                t.wait_ge(sEv.h, n - 1)
                            t.transpose(
                                ptr[bank][:],
                                xnt[:, buf * D + kc * 128:
                                    buf * D + kc * 128 + 128],
                                idb[:]).then_inc(sTr.h, 1)
                            sTr.n += 1

            @blk.gpsimd
            def _(g):
                if do_ln:
                    xr = xn_ap.rearrange("(kc q) t -> q kc t", q=128)
                    st3 = stage[:].rearrange("q (b kc t) -> q b kc t",
                                             b=2, kc=16)
                if do_add:
                    for st0 in range(2):
                        g.dma_start(
                            at[:, st0 * D:(st0 + 1) * D],
                            add_ap[st0 * 128:(st0 + 1) * 128, :]
                        ).then_inc(sAd[st0].h, 16)
                        sAd[st0].n += 16
                for st in range(32):
                    buf = st % 2
                    if do_add:
                        if st + 2 < 32:
                            g.wait_ge(sAs.h, st + 1)
                            g.dma_start(
                                at[:, buf * D:(buf + 1) * D],
                                add_ap[(st + 2) * 128:(st + 3) * 128, :]
                            ).then_inc(sAd[buf].h, 16)
                            sAd[buf].n += 16
                    if xout_ap is not None:
                        g.wait_ge(sAs.h, st + 1)
                        g.dma_start(
                            xout_ap[st * 128:(st + 1) * 128, :],
                            xs[:, buf * D:(buf + 1) * D]).then_inc(
                                sXo[buf].h, 16)
                        sXo[buf].n += 16
                    if do_ln and st % 4 == 3:
                        tb = st // 4
                        sbuf = tb % 2
                        g.wait_ge(sEv.h, (tb * 4 + 4) * 16)
                        g.dma_start(
                            xr[:, :, tb * TT:(tb + 1) * TT],
                            st3[:, sbuf, :, :]).then_inc(sSt[sbuf].h, 16)
                        sSt[sbuf].n += 16
                if xout_ap is not None:
                    g.wait_ge(sXo[0].h, sXo[0].n)
                    g.wait_ge(sXo[1].h, sXo[1].n)
                if do_ln:
                    g.wait_ge(sSt[0].h, sSt[0].n)
                    g.wait_ge(sSt[1].h, sSt[1].n)


class EvSys:
    def __init__(self, nc, tag):
        self.nc = nc
        self.tag = tag
        self.sems = {}
        self.cnt = {}
        self.val = {}

    def sem(self, name):
        if name not in self.sems:
            self.sems[name] = Sem(self.nc, f"{self.tag}{name}")
        return self.sems[name]

    def define(self, key, semname, amount=1):
        self.sem(semname)
        c = self.cnt.get(semname, 0) + amount
        self.cnt[semname] = c
        assert key not in self.val, key
        self.val[key] = (semname, c, amount)

    def inc(self, ins, key):
        semname, c, amount = self.val[key]
        ins.then_inc(self.sems[semname].h, amount)
        return ins

    def wait(self, eng, key):
        semname, c, amount = self.val[key]
        eng.wait_ge(self.sems[semname].h, c)


def fence(eng, sem, ins):
    ins.then_inc(sem.h, 1)
    sem.n += 1
    eng.wait_ge(sem.h, sem.n)


def emit_proj_a(p):
    nc = p.nc
    E = EvSys(nc, "p1")
    NMC = 8
    groups = []
    for g in range(2):
        groups.append(("Wqkv", g * 1024, "q", g * 8))
    for g in range(2):
        groups.append(("Wqkv", 2048 + g * 1024, "k", 16 + g * 8))
    for g in range(4):
        groups.append(("Wqkv", 4096 + g * 1024, "v", 32 + g * 8))
    for g in range(4):
        groups.append(("Wz", g * 1024, "z", 64 + g * 8))
    NG = len(groups)

    def sidx(gi, tt, mc):
        return (gi * NT + tt) * NMC + mc

    NSTEP = NG * NT * NMC
    kind_of = {sidx(gi, tt, mc): groups[gi][2]
               for gi in range(NG) for tt in range(NT) for mc in range(NMC)}

    consts = ["c_ones_col_bf", "c_ones_1x128_bf", "c_ones_1x128_f32",
              "c_row1_eps6", "c_row1_lnqs", "conv_w_rs", "A_log", "dt_bias",
              "c_id_f32"]
    for i in range(len(consts)):
        E.define(("c", i), "c", 16)
    for u in range(NG * NT + NT):
        E.define(("xt", u), f"xt{u % 2}", 16)
    for gi in range(NG):
        E.define(("wg", gi), f"wg{gi % 2}", 16)
    E.define(("wab",), "wg0", 16)
    for gi in range(NG):
        kind = groups[gi][2]
        for tt in range(NT):
            for mc in range(NMC):
                E.define(("main", sidx(gi, tt, mc)), "t", 1)
            if kind in ("q", "k"):
                for mc in range(NMC):
                    E.define(("ssmm", gi, tt, mc), "ts", 1)
                for mc in range(NMC):
                    E.define(("bcmm", gi, tt, mc), "tb", 1)
    for gi in range(NG):
        kind = groups[gi][2]
        for tt in range(NT):
            for mc in range(NMC):
                n = sidx(gi, tt, mc)
                if kind in ("q", "k", "v"):
                    E.define(("conv", n), "v1", 1)
                if kind in ("q", "k"):
                    E.define(("silu", n), "s1", 1)
                    E.define(("sq", n), "s2", 1)
                    E.define(("rows", gi, tt, mc), "v2", 1)
                    E.define(("out", n), "v3", 1)
                elif kind == "v":
                    E.define(("out", n), "s1", 1)
                else:
                    E.define(("out", n), "sz", 1)
            if kind in ("q", "k"):
                E.define(("rstd", gi, tt), "s3", 1)
    for n in range(NSTEP):
        E.define(("store", n), f"o{n % 4}", 16)
    E.define(("abinit_s",), "sab", 1)
    E.define(("abinit_v1",), "vab", 1)
    E.define(("abinit_t",), "tab", 1)
    E.define(("abinit_v2",), "vab", 1)
    for tc in range(NCH):
        E.define(("abmm", tc), "tab", 1)
        E.define(("abv1", tc), "vab", 1)
        E.define(("absp", tc), "sab", 1)
        E.define(("abe", tc), "sab", 1)
        E.define(("abg", tc), "vab", 1)
        E.define(("abb", tc), "vab", 1)
        E.define(("abtr", tc), "tab", 1)
        E.define(("abev", tc), "sab", 1)
        E.define(("abst", tc), f"bst{tc % 2}", 16)

    W = {}

    def wadd(site, key):
        W.setdefault(site, []).append(key)

    last_qk = {}
    prev_qk_gt = None
    for gi in range(NG):
        kind = groups[gi][2]
        for tt in range(NT):
            if kind in ("q", "k"):
                last_qk[(gi, tt)] = prev_qk_gt
                prev_qk_gt = (gi, tt)
    prev_ss = None
    prev_bc = None
    for gi in range(NG):
        kind = groups[gi][2]
        for tt in range(NT):
            for mc in range(NMC):
                n = sidx(gi, tt, mc)
                if n >= 2:
                    k2 = kind_of[n - 2]
                    wadd(("mainw", n),
                         ("conv", n - 2) if k2 in ("q", "k", "v")
                         else ("out", n - 2))
                if kind in ("q", "k", "v"):
                    if n >= NMC and kind_of[n - NMC] in ("q", "k", "v"):
                        k2 = kind_of[n - NMC]
                        wadd(("convw", n),
                             ("silu", n - NMC) if k2 in ("q", "k")
                             else ("out", n - NMC))
                if kind in ("q", "k"):
                    if n >= NMC and kind_of[n - NMC] in ("q", "k"):
                        wadd(("siluw", n), ("out", n - NMC))
                        wadd(("sqw", n), ("ssmm", gi, tt - 1, mc) if tt > 0
                             else ("ssmm", gi - 1, NT - 1, mc))
                if n >= 4:
                    wadd(("outw", n), ("store", n - 4))
            if kind in ("q", "k"):
                for mc in range(NMC):
                    if prev_ss is not None:
                        wadd(("ssmmw", gi, tt, mc), prev_ss)
                    prev_ss = ("rows", gi, tt, mc)
                lq = last_qk[(gi, tt)]
                if lq is not None:
                    wadd(("rowsw", gi, tt), ("rstd",) + lq)
                    wadd(("rstdw", gi, tt), ("bcmm",) + lq + (NMC - 1,))
                for mc in range(NMC):
                    if prev_bc is not None:
                        wadd(("bcmmw", gi, tt, mc), prev_bc)
                    prev_bc = ("out", sidx(gi, tt, mc))
    wadd(("abinitw",), prev_bc)

    sFs = Sem(nc, "p1fs")
    sFv = Sem(nc, "p1fv")
    g_sb = p.sb["g_sb"]
    beta_sb = p.sb["beta_sb"]

    import contextlib
    with contextlib.ExitStack() as _st:
        wg = _st.enter_context(nc.sbuf_tensor([128, 2 * 16 * 1024], BF16))
        xnt = _st.enter_context(nc.sbuf_tensor([128, 2 * 16 * TT], BF16))
        conv_sb = _st.enter_context(nc.sbuf_tensor([128, NMC * TT], F32))
        silu_bf = _st.enter_context(nc.sbuf_tensor([128, NMC * TT], BF16))
        sq_bf = _st.enter_context(nc.sbuf_tensor([128, NMC * TT], BF16))
        rows_sb = _st.enter_context(nc.sbuf_tensor([1, NMC * TT], F32))
        lnrows = _st.enter_context(nc.sbuf_tensor([1, NMC * TT], F32))
        rstd_bf = _st.enter_context(nc.sbuf_tensor([1, NMC * TT], BF16))
        outst = _st.enter_context(nc.sbuf_tensor([128, 4 * TT], BF16))
        tails = _st.enter_context(nc.sbuf_tensor([128, NMC * 8], F32))
        convw = _st.enter_context(nc.sbuf_tensor([128, 64 * 4], F32))
        onescol = _st.enter_context(nc.sbuf_tensor([128, 1], BF16))
        onesrow = _st.enter_context(nc.sbuf_tensor([1, 128], BF16))
        onesrow_f = _st.enter_context(nc.sbuf_tensor([1, 128], F32))
        r1eps = _st.enter_context(nc.sbuf_tensor([1, 1], F32))
        r1lnq = _st.enter_context(nc.sbuf_tensor([1, 1], F32))
        wab = _st.enter_context(nc.sbuf_tensor([128, 16 * 64], BF16))
        alog_r = _st.enter_context(nc.sbuf_tensor([1, 32], F32))
        dtb_r = _st.enter_context(nc.sbuf_tensor([1, 32], F32))
        expa_r = _st.enter_context(nc.sbuf_tensor([1, 32], F32))
        nega_r = _st.enter_context(nc.sbuf_tensor([1, 32], F32))
        negexpA_bc = _st.enter_context(nc.sbuf_tensor([128, 32], F32))
        dtb_bc = _st.enter_context(nc.sbuf_tensor([128, 32], F32))
        ta_t = _st.enter_context(nc.sbuf_tensor([128, 32], F32))
        tb_t = _st.enter_context(nc.sbuf_tensor([128, 32], F32))
        tb2_t = _st.enter_context(nc.sbuf_tensor([128, 32], F32))
        tb3_t = _st.enter_context(nc.sbuf_tensor([128, 32], F32))
        tc_t = _st.enter_context(nc.sbuf_tensor([128, 32], F32))
        sp_t = _st.enter_context(nc.sbuf_tensor([128, 32], F32))
        e_t = _st.enter_context(nc.sbuf_tensor([128, 32], F32))
        d_t = _st.enter_context(nc.sbuf_tensor([128, 32], F32))
        btr_sb = _st.enter_context(nc.sbuf_tensor([32, 2 * 128], F32))
        idf = _st.enter_context(nc.sbuf_tensor([128, 128], F32))
        ps0 = _st.enter_context(nc.psum_tensor([128, 512], F32))
        ps1 = _st.enter_context(nc.psum_tensor([128, 512], F32))
        psr = _st.enter_context(nc.psum_tensor([1, 512], F32))
        psb = _st.enter_context(nc.psum_tensor([128, 512], F32))
        psab = _st.enter_context(nc.psum_tensor([128, 64], F32))
        pstr = _st.enter_context(nc.psum_tensor([32, 128], F32))
        psm = [ps0, ps1]
        xnr = p.d["xn1"].rearrange("(kc q) t -> q kc t", q=128)
        xnt3 = xnt[:].rearrange("q (b kc t) -> q b kc t", b=2, kc=16)
        wg3 = wg[:].rearrange("q (b kc m) -> q b kc m", b=2, kc=16)
        dst_of = {"q": (p.d["qa"], 0), "k": (p.d["ka"], 16),
                  "v": (p.d["va"], 32), "z": (p.d["za"], 64)}
        const_src = [(onescol, "c_ones_col_bf"), (onesrow, "c_ones_1x128_bf"),
                     (onesrow_f, "c_ones_1x128_f32"), (r1eps, "c_row1_eps6"),
                     (r1lnq, "c_row1_lnqs"), (convw, "conv_w_rs"),
                     (alog_r, "A_log"), (dtb_r, "dt_bias"), (idf, "c_id_f32")]

        def wsite(eng, site):
            for k in W.get(site, []):
                E.wait(eng, k)

        with nc.Block() as blk:
            @blk.sync
            def _(sync):
                for i, (dst, srcn) in enumerate(const_src):
                    E.inc(sync.dma_start(dst[:], p.d[srcn][:]), ("c", i))
                for u in range(NG * NT):
                    gi, tt = divmod(u, NT)
                    slot = u % 2
                    if u >= 2:
                        E.wait(sync, ("main", sidx(*divmod(u - 2, NT), NMC - 1)))
                    E.inc(sync.dma_start(
                        xnt3[:, slot, :, :],
                        xnr[:, :, tt * TT:(tt + 1) * TT]), ("xt", u))
                for tt in range(NT):
                    u = NG * NT + tt
                    if tt < 2:
                        g2, t2 = divmod(u - 2, NT)
                        E.wait(sync, ("main", sidx(g2, t2, NMC - 1)))
                    else:
                        E.wait(sync, ("abmm", (tt - 2) * 4 + 3))
                    E.inc(sync.dma_start(
                        xnt3[:, u % 2, :, :],
                        xnr[:, :, tt * TT:(tt + 1) * TT]), ("xt", u))

            @blk.gpsimd
            def _(g):
                for gi2 in range(2):
                    wn, col0, kind, fc0 = groups[gi2]
                    wr = p.d[wn].rearrange("(kc q) m -> q kc m", q=128)
                    E.inc(g.dma_start(wg3[:, gi2 % 2, :, :],
                                      wr[:, :, col0:col0 + 1024]), ("wg", gi2))
                for n in range(NSTEP):
                    gi, r = divmod(n, NT * NMC)
                    tt, mc = divmod(r, NMC)
                    kind = kind_of[n]
                    E.wait(g, ("out", n))
                    dst, base = dst_of[kind]
                    fcl = (groups[gi][3] + mc) - base
                    slot = n % 4
                    E.inc(g.dma_start(
                        dst[fcl * 128:(fcl + 1) * 128, tt * TT:(tt + 1) * TT],
                        outst[:, slot * TT:(slot + 1) * TT]), ("store", n))
                    if tt == NT - 1 and mc == NMC - 1 and gi + 2 < NG:
                        gi2 = gi + 2
                        wn2, col02 = groups[gi2][0], groups[gi2][1]
                        E.wait(g, ("main", sidx(gi2 - 2, NT - 1, NMC - 1)))
                        wr = p.d[wn2].rearrange("(kc q) m -> q kc m", q=128)
                        E.inc(g.dma_start(wg3[:, gi2 % 2, :, :],
                                          wr[:, :, col02:col02 + 1024]),
                              ("wg", gi2))
                E.wait(g, ("main", NSTEP - 1))
                wabr = p.d["Wab"].rearrange("(kc q) m -> q kc m", q=128)
                E.inc(g.dma_start(wab[:].rearrange("q (kc m) -> q kc m", kc=16),
                                  wabr), ("wab",))
                for tc in range(NCH):
                    E.wait(g, ("abev", tc))
                    E.inc(g.dma_start(
                        p.d["beta_T"][:, tc * 128:(tc + 1) * 128],
                        btr_sb[:, (tc % 2) * 128:(tc % 2 + 1) * 128]),
                        ("abst", tc))
                for i in range(4):
                    g.wait_ge(E.sems[f"o{i}"].h, E.cnt[f"o{i}"])
                g.wait_ge(E.sems["bst0"].h, 16 * (NCH // 2))
                g.wait_ge(E.sems["bst1"].h, 16 * (NCH // 2))

            @blk.tensor
            def _(t):
                E.wait(t, ("c", len(consts) - 1))
                t.wait_ge(E.sems["c"].h, 16 * len(consts))
                for gi in range(NG):
                    wn, col0, kind, fc0 = groups[gi]
                    wslot = gi % 2
                    E.wait(t, ("wg", gi))
                    for tt in range(NT):
                        u = gi * NT + tt
                        xslot = u % 2
                        E.wait(t, ("xt", u))
                        for mc in range(NMC):
                            n = sidx(gi, tt, mc)
                            bank = n % 2
                            wsite(t, ("mainw", n))
                            for kc in range(16):
                                ins = t.matmul(
                                    psm[bank][:],
                                    wg[:, wslot * 16384 + kc * 1024 + mc * 128:
                                       wslot * 16384 + kc * 1024 + mc * 128 + 128],
                                    xnt[:, xslot * 16 * TT + kc * TT:
                                        xslot * 16 * TT + (kc + 1) * TT],
                                    start=(kc == 0), stop=(kc == 15))
                            E.inc(ins, ("main", n))
                        if kind in ("q", "k"):
                            for mc in range(NMC):
                                n = sidx(gi, tt, mc)
                                E.wait(t, ("sq", n))
                                wsite(t, ("ssmmw", gi, tt, mc))
                                E.inc(t.matmul(psr[:], onescol[:],
                                               sq_bf[:, mc * TT:(mc + 1) * TT],
                                               start=True, stop=True),
                                      ("ssmm", gi, tt, mc))
                            E.wait(t, ("rstd", gi, tt))
                            for mc in range(NMC):
                                wsite(t, ("bcmmw", gi, tt, mc))
                                E.inc(t.matmul(
                                    psb[:], onesrow[:],
                                    rstd_bf[0:1, mc * TT:(mc + 1) * TT],
                                    start=True, stop=True),
                                    ("bcmm", gi, tt, mc))
                E.wait(t, ("wab",))
                E.wait(t, ("abinit_v1",))
                wsite(t, ("abinitw",))
                t.matmul(psb[:, 0:32], onesrow_f[:], nega_r[:],
                         start=True, stop=True)
                E.inc(t.matmul(psb[:, 32:64], onesrow_f[:], dtb_r[:],
                               start=True, stop=True), ("abinit_t",))
                for tc in range(NCH):
                    tt, sub = divmod(tc, 4)
                    u = NG * NT + tt
                    xslot = u % 2
                    E.wait(t, ("xt", u))
                    if tc == 0:
                        E.wait(t, ("abinit_v2",))
                    if tc >= 1:
                        E.wait(t, ("abv1", tc - 1))
                        E.wait(t, ("abe", tc - 1))
                    for kc in range(16):
                        ins = t.matmul(
                            psab[:],
                            xnt[:, xslot * 16 * TT + kc * TT + sub * 128:
                                xslot * 16 * TT + kc * TT + sub * 128 + 128],
                            wab[:, kc * 64:(kc + 1) * 64],
                            start=(kc == 0), stop=(kc == 15))
                    E.inc(ins, ("abmm", tc))
                    E.wait(t, ("abb", tc))
                    if tc >= 1:
                        E.wait(t, ("abev", tc - 1))
                    E.inc(t.transpose(pstr[:],
                                      beta_sb[:, tc * 32:(tc + 1) * 32],
                                      idf[:]), ("abtr", tc))

            @blk.vector
            def _(v):
                for gi in range(NG):
                    wn, col0, kind, fc0 = groups[gi]
                    if kind == "z":
                        continue
                    for tt in range(NT):
                        for mc in range(NMC):
                            n = sidx(gi, tt, mc)
                            fc = fc0 + mc
                            pm = psm[n % 2]
                            cur, nxt = tt % 2, (tt + 1) % 2
                            tcur = tails[:, mc * 8 + cur * 3:
                                         mc * 8 + cur * 3 + 3]
                            tnxt = tails[:, mc * 8 + nxt * 3:
                                         mc * 8 + nxt * 3 + 3]
                            cv = conv_sb[:, mc * TT:(mc + 1) * TT]
                            w = [convw[:, fc * 4 + i:fc * 4 + i + 1]
                                 for i in range(4)]
                            E.wait(v, ("main", n))
                            if tt in (0, 4):
                                fence(v, sFv, v.memset(tcur, 0.0))
                            fence(v, sFv, v.tensor_copy(tnxt, pm[:, 509:512]))
                            wsite(v, ("convw", n))
                            fence(v, sFv, v.tensor_scalar_mul(cv, pm[:], w[3]))
                            fence(v, sFv, v.scalar_tensor_tensor(
                                cv[:, 1:512], pm[:, 0:511], w[2], cv[:, 1:512],
                                ALU.mult, ALU.add))
                            fence(v, sFv, v.scalar_tensor_tensor(
                                cv[:, 2:512], pm[:, 0:510], w[1], cv[:, 2:512],
                                ALU.mult, ALU.add))
                            fence(v, sFv, v.scalar_tensor_tensor(
                                cv[:, 3:512], pm[:, 0:509], w[0], cv[:, 3:512],
                                ALU.mult, ALU.add))
                            fence(v, sFv, v.scalar_tensor_tensor(
                                cv[:, 0:3], tcur, w[0], cv[:, 0:3],
                                ALU.mult, ALU.add))
                            fence(v, sFv, v.scalar_tensor_tensor(
                                cv[:, 0:2], tcur[:, 1:3], w[1], cv[:, 0:2],
                                ALU.mult, ALU.add))
                            E.inc(v.scalar_tensor_tensor(
                                cv[:, 0:1], tcur[:, 2:3], w[2], cv[:, 0:1],
                                ALU.mult, ALU.add), ("conv", n))
                        if kind in ("q", "k"):
                            for mc in range(NMC):
                                E.wait(v, ("ssmm", gi, tt, mc))
                                wsite(v, ("rowsw", gi, tt))
                                E.inc(v.tensor_copy(
                                    rows_sb[0:1, mc * TT:(mc + 1) * TT],
                                    psr[0:1, :]), ("rows", gi, tt, mc))
                            for mc in range(NMC):
                                n = sidx(gi, tt, mc)
                                E.wait(v, ("bcmm", gi, tt, mc))
                                wsite(v, ("outw", n))
                                slot = n % 4
                                E.inc(v.tensor_mul(
                                    outst[:, slot * TT:(slot + 1) * TT],
                                    silu_bf[:, mc * TT:(mc + 1) * TT],
                                    psb[:]), ("out", n))
                E.wait(v, ("abinit_s",))
                E.inc(v.tensor_scalar_mul(nega_r[:], expa_r[:], -1.0),
                      ("abinit_v1",))
                E.wait(v, ("abinit_t",))
                fence(v, sFv, v.tensor_copy(negexpA_bc[:], psb[:, 0:32]))
                E.inc(v.tensor_copy(dtb_bc[:], psb[:, 32:64]), ("abinit_v2",))
                v.wait_ge(E.sems["vab"].h, E.val[("abinit_v2",)][1])
                for tc in range(NCH):
                    E.wait(v, ("abmm", tc))
                    ins = v.tensor_add(ta_t[:], psab[:, 0:32], dtb_bc[:])
                    E.inc(ins, ("abv1", tc))
                    v.wait_ge(E.sems["vab"].h, E.val[("abv1", tc)][1])
                    fence(v, sFv, v.tensor_scalar_max(tc_t[:], ta_t[:], 0.0))
                    E.wait(v, ("absp", tc))
                    fence(v, sFv, v.tensor_add(sp_t[:], tb3_t[:], tc_t[:]))
                    E.inc(v.tensor_mul(g_sb[:, tc * 32:(tc + 1) * 32],
                                       sp_t[:], negexpA_bc[:]), ("abg", tc))
                    E.wait(v, ("abe", tc))
                    fence(v, sFv, v.tensor_scalar_add(d_t[:], e_t[:], 1.0))
                    E.inc(v.reciprocal(beta_sb[:, tc * 32:(tc + 1) * 32],
                                       d_t[:]), ("abb", tc))

            @blk.scalar
            def _(s):
                for gi in range(NG):
                    wn, col0, kind, fc0 = groups[gi]
                    for tt in range(NT):
                        for mc in range(NMC):
                            n = sidx(gi, tt, mc)
                            if kind in ("q", "k"):
                                E.wait(s, ("conv", n))
                                wsite(s, ("siluw", n))
                                ins = s.activation(
                                    silu_bf[:, mc * TT:(mc + 1) * TT],
                                    conv_sb[:, mc * TT:(mc + 1) * TT], AF.Silu)
                                E.inc(ins, ("silu", n))
                                E.wait(s, ("silu", n))
                                wsite(s, ("sqw", n))
                                E.inc(s.activation(
                                    sq_bf[:, mc * TT:(mc + 1) * TT],
                                    silu_bf[:, mc * TT:(mc + 1) * TT],
                                    AF.Square), ("sq", n))
                            elif kind == "v":
                                E.wait(s, ("conv", n))
                                wsite(s, ("outw", n))
                                slot = n % 4
                                E.inc(s.activation(
                                    outst[:, slot * TT:(slot + 1) * TT],
                                    conv_sb[:, mc * TT:(mc + 1) * TT],
                                    AF.Silu), ("out", n))
                            else:
                                E.wait(s, ("main", n))
                                wsite(s, ("outw", n))
                                slot = n % 4
                                E.inc(s.copy(
                                    outst[:, slot * TT:(slot + 1) * TT],
                                    psm[n % 2][:]), ("out", n))
                        if kind in ("q", "k"):
                            E.wait(s, ("rows", gi, tt, NMC - 1))
                            wsite(s, ("rstdw", gi, tt))
                            ins = s.activation(lnrows[:], rows_sb[:], AF.Ln,
                                               bias=r1eps[:])
                            fence(s, sFs, ins)
                            if kind == "q":
                                ins = s.activation(rstd_bf[:], lnrows[:],
                                                   AF.Exp, scale=-0.5,
                                                   bias=r1lnq[:])
                            else:
                                ins = s.activation(rstd_bf[:], lnrows[:],
                                                   AF.Exp, scale=-0.5)
                            E.inc(ins, ("rstd", gi, tt))
                s.wait_ge(E.sems["c"].h, E.cnt["c"])
                E.inc(s.activation(expa_r[:], alog_r[:], AF.Exp),
                      ("abinit_s",))
                for tc in range(NCH):
                    E.wait(s, ("abv1", tc))
                    ins = s.activation(tb_t[:], ta_t[:], AF.Abs)
                    fence(s, sFs, ins)
                    ins = s.activation(tb2_t[:], tb_t[:], AF.Exp, scale=-1.0)
                    fence(s, sFs, ins)
                    E.inc(s.activation(tb3_t[:], tb2_t[:], AF.Ln, bias=1.0),
                          ("absp", tc))
                    E.inc(s.activation(e_t[:], psab[:, 32:64], AF.Exp,
                                       scale=-1.0), ("abe", tc))
                    E.wait(s, ("abtr", tc))
                    if tc >= 2:
                        E.wait(s, ("abst", tc - 2))
                    E.inc(s.copy(btr_sb[:, (tc % 2) * 128:(tc % 2 + 1) * 128],
                                 pstr[:]), ("abev", tc))


def emit_ffn_a(p, xn_in, tag):
    nc = p.nc
    E = EvSys(nc, tag)
    NIG, NICG = 16, 4
    NSA = NIG * NT * NICG
    NWL = 2 * NICG
    NQD = 4 * NWL

    def sidx(ig, tt, ic):
        return (ig * NT + tt) * NICG + ic

    for i in range(3):
        E.define(("c", i), "c", 16)
    for ig in range(NIG):
        for j in range(NWL):
            E.define(("wld", ig, j), f"wl{(ig * NWL + j) % 2}", 16)
        for j in range(NWL):
            E.define(("deq", ig, j), "vd", 1)
        for qn in range(NQD):
            E.define(("wtr", ig, qn), "tw", 1)
            E.define(("wev", ig, qn), "sw", 1)
    for u in range(NIG * NT):
        E.define(("xt", u), f"xt{u % 2}", 16)
    for n in range(NSA):
        E.define(("gmm", n), "t", 1)
        E.define(("umm", n), "t", 1)
        E.define(("sg", n), "ss", 1)
        E.define(("hmul", n), "vh", 1)
        E.define(("hst", n), f"o{n % 4}", 16)

    import contextlib
    with contextlib.ExitStack() as _st:
        en = _st.enter_context
        gsc = en(nc.sbuf_tensor(f"{tag}_gsc", [128, 64], F32))
        usc = en(nc.sbuf_tensor(f"{tag}_usc", [128, 64], F32))
        idb = en(nc.sbuf_tensor(f"{tag}_idb", [128, 128], BF16))
        wint = en(nc.sbuf_tensor(f"{tag}_wint", [128, 2 * D], I32))
        wdeq = en(nc.sbuf_tensor(f"{tag}_wdeq", [128, 2 * D], BF16))
        wgt = en(nc.sbuf_tensor(f"{tag}_wgt", [128, 2 * NICG * 16 * 128], BF16))
        wut = en(nc.sbuf_tensor(f"{tag}_wut", [128, 2 * NICG * 16 * 128], BF16))
        xnt = en(nc.sbuf_tensor(f"{tag}_xnt", [128, 2 * 16 * TT], BF16))
        sg_bf = en(nc.sbuf_tensor(f"{tag}_sg", [128, 2 * TT], BF16))
        outst = en(nc.sbuf_tensor(f"{tag}_outst", [128, 4 * TT], BF16))
        psg = [en(nc.psum_tensor(f"{tag}_psg{i}", [128, 512], F32))
               for i in range(2)]
        psu = [en(nc.psum_tensor(f"{tag}_psu{i}", [128, 512], F32))
               for i in range(2)]
        pst = [en(nc.psum_tensor(f"{tag}_pst{i}", [128, 512], BF16))
               for i in range(2)]

        xnr = p.d[xn_in].rearrange("(kc q) t -> q kc t", q=128)
        xnt3 = xnt[:].rearrange("q (b kc t) -> q b kc t", b=2, kc=16)
        gre = p.d["gate_w"].rearrange("(ic q) h -> q ic h", q=128)
        ure = p.d["up_w"].rearrange("(ic q) h -> q ic h", q=128)
        hre = p.d["hbuf"].rearrange("(ic q) t -> q ic t", q=128)
        wi2 = wint[:].rearrange("q (b h) -> q b h", b=2)

        def wld_src(ig, j):
            ic, isup = divmod(j, 2)
            return (ure if isup else gre)[:, ig * NICG + ic, :]

        def wslot(ig):
            return (ig % 2) * NICG * 2048

        with nc.Block() as blk:
            @blk.sync
            def _(sync):
                E.inc(sync.dma_start(gsc[:], p.d["gate_scale_rs"][:]), ("c", 0))
                E.inc(sync.dma_start(usc[:], p.d["up_scale_rs"][:]), ("c", 1))
                E.inc(sync.dma_start(idb[:], p.d["c_id_bf"][:]), ("c", 2))
                for u in range(NIG * NT):
                    ig, tt = divmod(u, NT)
                    if u >= 2:
                        E.wait(sync, ("umm", sidx(*divmod(u - 2, NT), NICG - 1)))
                    E.inc(sync.dma_start(
                        xnt3[:, u % 2, :, :],
                        xnr[:, :, tt * TT:(tt + 1) * TT]), ("xt", u))

            @blk.gpsimd
            def _(g):
                def wld(ig, j):
                    gwli = ig * NWL + j
                    if gwli >= 2:
                        pig, pj = divmod(gwli - 2, NWL)
                        E.wait(g, ("deq", pig, pj))
                    E.inc(g.dma_start(wi2[:, gwli % 2, :], wld_src(ig, j)),
                          ("wld", ig, j))

                for j in range(NWL):
                    wld(0, j)
                for n in range(NSA):
                    ig, r = divmod(n, NT * NICG)
                    st = r
                    if st % 4 == 0 and ig + 1 < NIG:
                        wld(ig + 1, st // 4)
                    E.wait(g, ("hmul", n))
                    tt, ic = divmod(r, NICG)
                    icg = ig * NICG + ic
                    E.inc(g.dma_start(
                        hre[:, icg, tt * TT:(tt + 1) * TT],
                        outst[:, (n % 4) * TT:(n % 4 + 1) * TT]), ("hst", n))
                for i in range(4):
                    g.wait_ge(E.sems[f"o{i}"].h, E.cnt[f"o{i}"])

            @blk.vector
            def _(v):
                v.wait_ge(E.sems["c"].h, E.cnt["c"])

                def deq(ig, j):
                    gwli = ig * NWL + j
                    ic, isup = divmod(j, 2)
                    E.wait(v, ("wld", ig, j))
                    if gwli >= 2:
                        pig, pj = divmod(gwli - 2, NWL)
                        E.wait(v, ("wtr", pig, pj * 4 + 3))
                    scol = (usc if isup else gsc)[:, ig * NICG + ic:
                                                  ig * NICG + ic + 1]
                    E.inc(v.tensor_scalar(
                        wdeq[:, (gwli % 2) * D:(gwli % 2 + 1) * D],
                        wint[:, (gwli % 2) * D:(gwli % 2 + 1) * D],
                        -128.0, scol, ALU.add, ALU.mult), ("deq", ig, j))

                for j in range(NWL):
                    deq(0, j)
                for n in range(NSA):
                    ig, st = divmod(n, NT * NICG)
                    if st % 4 == 0 and ig + 1 < NIG:
                        deq(ig + 1, st // 4)
                    E.wait(v, ("umm", n))
                    E.wait(v, ("sg", n))
                    if n >= 4:
                        E.wait(v, ("hst", n - 4))
                    E.inc(v.tensor_mul(
                        outst[:, (n % 4) * TT:(n % 4 + 1) * TT],
                        sg_bf[:, (n % 2) * TT:(n % 2 + 1) * TT],
                        psu[n % 2][:]), ("hmul", n))

            @blk.tensor
            def _(t):
                t.wait_ge(E.sems["c"].h, E.cnt["c"])
                gq = [0]

                def quad(ig, qn):
                    j, b = divmod(qn, 4)
                    E.wait(t, ("deq", ig, j))
                    if gq[0] >= 2:
                        pgq = gq[0] - 2
                        E.wait(t, ("wev", pgq // NQD, pgq % NQD))
                    gwli = ig * NWL + j
                    for jj in range(4):
                        kc = b * 4 + jj
                        ins = t.transpose(
                            pst[gq[0] % 2][:, jj * 128:(jj + 1) * 128],
                            wdeq[:, (gwli % 2) * D + kc * 128:
                                 (gwli % 2) * D + kc * 128 + 128],
                            idb[:])
                    E.inc(ins, ("wtr", ig, qn))
                    gq[0] += 1

                for qn in range(NQD):
                    quad(0, qn)
                for n in range(NSA):
                    ig, st = divmod(n, NT * NICG)
                    tt, ic = divmod(st, NICG)
                    u = ig * NT + tt
                    if st == 0:
                        E.wait(t, ("wev", ig, NQD - 1))
                    if ic == 0:
                        E.wait(t, ("xt", u))
                    bank = n % 2
                    if n >= 2:
                        E.wait(t, ("sg", n - 2))
                    for kc in range(16):
                        ins = t.matmul(
                            psg[bank][:],
                            wgt[:, wslot(ig) + (ic * 16 + kc) * 128:
                                wslot(ig) + (ic * 16 + kc) * 128 + 128],
                            xnt[:, (u % 2) * 16 * TT + kc * TT:
                                (u % 2) * 16 * TT + (kc + 1) * TT],
                            start=(kc == 0), stop=(kc == 15))
                    E.inc(ins, ("gmm", n))
                    if n >= 2:
                        E.wait(t, ("hmul", n - 2))
                    for kc in range(16):
                        ins = t.matmul(
                            psu[bank][:],
                            wut[:, wslot(ig) + (ic * 16 + kc) * 128:
                                wslot(ig) + (ic * 16 + kc) * 128 + 128],
                            xnt[:, (u % 2) * 16 * TT + kc * TT:
                                (u % 2) * 16 * TT + (kc + 1) * TT],
                            start=(kc == 0), stop=(kc == 15))
                    E.inc(ins, ("umm", n))
                    if ig + 1 < NIG:
                        quad(ig + 1, st)

            @blk.scalar
            def _(s):
                gq = [0]

                def evac(ig, qn):
                    j, b = divmod(qn, 4)
                    ic, isup = divmod(j, 2)
                    dstw = wut if isup else wgt
                    E.wait(s, ("wtr", ig, qn))
                    if qn == 0 and ig >= 2:
                        E.wait(s, ("umm", sidx(ig - 2, NT - 1, NICG - 1)))
                    E.inc(s.copy(
                        dstw[:, wslot(ig) + ic * 2048 + b * 512:
                             wslot(ig) + ic * 2048 + b * 512 + 512],
                        pst[gq[0] % 2][:]), ("wev", ig, qn))
                    gq[0] += 1

                for qn in range(NQD):
                    evac(0, qn)
                for n in range(NSA):
                    ig, st = divmod(n, NT * NICG)
                    E.wait(s, ("gmm", n))
                    if n >= 2:
                        E.wait(s, ("hmul", n - 2))
                    E.inc(s.activation(
                        sg_bf[:, (n % 2) * TT:(n % 2 + 1) * TT],
                        psg[n % 2][:], AF.Silu), ("sg", n))
                    if ig + 1 < NIG:
                        evac(ig + 1, st)


def emit_ffn_b(p, out_ap, tag):
    nc = p.nc
    E = EvSys(nc, tag)
    NQ, KCH = 4, 16

    for i in range(2):
        E.define(("c", i), "c", 16)
    bank_rel = {}
    gl = 0
    for q in range(NQ):
        for wl in range(16):
            E.define(("bwld", q, wl), f"bw{wl % 2}", 16)
            E.define(("bdeq", q, wl), "vd", 1)
            for b in range(4):
                E.define(("btr", q, wl, b), "tw", 1)
                E.define(("bev", q, wl, b), "sw", 1)
        for tt in range(NT):
            E.define(("bht", q, tt), f"bh{tt % 2}", 16)
            for m in range(4):
                if q > 0:
                    E.define(("bpl", q, tt, m), f"bp{(tt * 4 + m) % 2}", 16)
                for cn in range(4):
                    E.define(("bmm", q, tt, m, cn), "t", 1)
                    E.define(("bev2", q, tt, m, cn), "vb", 1)
                    bank_rel[gl] = ("bev2", q, tt, m, cn)
                    gl += 1
                E.define(("bpst", q, tt, m), f"po{(tt * 4 + m) % 2}", 16)

    import contextlib
    with contextlib.ExitStack() as _st:
        en = _st.enter_context
        dsc = en(nc.sbuf_tensor(f"{tag}_dsc", [128, 16], F32))
        idb = en(nc.sbuf_tensor(f"{tag}_idb", [128, 128], BF16))
        bint = en(nc.sbuf_tensor(f"{tag}_bint", [128, 2 * 2048], I32))
        bdeq = en(nc.sbuf_tensor(f"{tag}_bdeq", [128, 2 * 2048], BF16))
        dtq = en(nc.sbuf_tensor(f"{tag}_dtq", [128, KCH * D], BF16))
        hti = en(nc.sbuf_tensor(f"{tag}_hti", [128, 2 * KCH * TT], BF16))
        prt = en(nc.sbuf_tensor(f"{tag}_prt", [128, 2 * D], F32))
        pout = en(nc.sbuf_tensor(f"{tag}_pout", [128, 2 * D], F32))
        pst = [en(nc.psum_tensor(f"{tag}_pst{i}", [128, 512], BF16))
               for i in range(2)]
        psd = [en(nc.psum_tensor(f"{tag}_psd{i}", [128, 512], F32))
               for i in range(2)]

        hre = p.d["hbuf"].rearrange("(ic q) t -> q ic t", q=128)
        hti3 = hti[:].rearrange("q (b ic t) -> q b ic t", b=2, ic=KCH)
        dre = p.d["down_w"].rearrange("(cc q) i -> q cc i", q=128)

        with nc.Block() as blk:
            @blk.sync
            def _(sync):
                E.inc(sync.dma_start(dsc[:], p.d["down_scale_rs"][:]), ("c", 0))
                E.inc(sync.dma_start(idb[:], p.d["c_id_bf"][:]), ("c", 1))
                for q in range(NQ):
                    for tt in range(NT):
                        u = q * NT + tt
                        if u >= 2:
                            q2, t2 = divmod(u - 2, NT)
                            E.wait(sync, ("bmm", q2, t2, 3, 3))
                        E.inc(sync.dma_start(
                            hti3[:, tt % 2, :, :],
                            hre[:, q * KCH:(q + 1) * KCH,
                                tt * TT:(tt + 1) * TT]), ("bht", q, tt))
                        if q > 0:
                            for m in range(4):
                                st = tt * 4 + m
                                if st >= 2 or q > 1:
                                    pq, pst_ = (q, st - 2) if st >= 2 \
                                        else (q - 1, 30 + st)
                                    E.wait(sync, ("bev2", pq, pst_ // 4,
                                                  pst_ % 4, 3))
                                E.wait(sync, ("bpst", q - 1, tt, m))
                                E.inc(sync.dma_start(
                                    prt[:, (st % 2) * D:(st % 2 + 1) * D],
                                    out_ap[(tt * 4 + m) * 128:
                                           (tt * 4 + m) * 128 + 128, :]),
                                    ("bpl", q, tt, m))

            @blk.gpsimd
            def _(g):
                for q in range(NQ):
                    for wl in range(16):
                        if q * 16 + wl >= 2:
                            pq, pwl = divmod(q * 16 + wl - 2, 16)
                            E.wait(g, ("bdeq", pq, pwl))
                        E.inc(g.dma_start(
                            bint[:].rearrange("q (b i) -> q b i",
                                              b=2)[:, wl % 2, :],
                            dre[:, wl, q * 2048:(q + 1) * 2048]),
                            ("bwld", q, wl))
                    for tt in range(NT):
                        for m in range(4):
                            st = tt * 4 + m
                            E.wait(g, ("bev2", q, tt, m, 3))
                            E.inc(g.dma_start(
                                out_ap[(tt * 4 + m) * 128:
                                       (tt * 4 + m) * 128 + 128, :],
                                pout[:, (st % 2) * D:(st % 2 + 1) * D]),
                                ("bpst", q, tt, m))
                g.wait_ge(E.sems["po0"].h, E.cnt["po0"])
                g.wait_ge(E.sems["po1"].h, E.cnt["po1"])

            @blk.vector
            def _(v):
                v.wait_ge(E.sems["c"].h, E.cnt["c"])
                gl = 0
                for q in range(NQ):
                    for wl in range(16):
                        E.wait(v, ("bwld", q, wl))
                        if q * 16 + wl >= 2:
                            pq, pwl = divmod(q * 16 + wl - 2, 16)
                            E.wait(v, ("btr", pq, pwl, 3))
                        E.inc(v.tensor_scalar(
                            bdeq[:, (wl % 2) * 2048:(wl % 2 + 1) * 2048],
                            bint[:, (wl % 2) * 2048:(wl % 2 + 1) * 2048],
                            -128.0, dsc[:, wl:wl + 1], ALU.add, ALU.mult),
                            ("bdeq", q, wl))
                    for tt in range(NT):
                        for m in range(4):
                            st = tt * 4 + m
                            sl = st % 2
                            for cn in range(4):
                                E.wait(v, ("bmm", q, tt, m, cn))
                                if cn == 0 and (q * 32 + st) >= 2:
                                    pq, pst_ = divmod(q * 32 + st - 2, 32)
                                    E.wait(v, ("bpst", pq, pst_ // 4,
                                               pst_ % 4))
                                if q == 0:
                                    ins = v.tensor_copy(
                                        pout[:, sl * D + cn * 512:
                                             sl * D + cn * 512 + 512],
                                        psd[gl % 2][:])
                                else:
                                    if cn == 0:
                                        E.wait(v, ("bpl", q, tt, m))
                                    ins = v.tensor_add(
                                        pout[:, sl * D + cn * 512:
                                             sl * D + cn * 512 + 512],
                                        prt[:, sl * D + cn * 512:
                                            sl * D + cn * 512 + 512],
                                        psd[gl % 2][:])
                                E.inc(ins, ("bev2", q, tt, m, cn))
                                gl += 1

            @blk.tensor
            def _(t):
                t.wait_ge(E.sems["c"].h, E.cnt["c"])
                gl = 0
                qidx = 0
                for q in range(NQ):
                    for wl in range(16):
                        E.wait(t, ("bdeq", q, wl))
                        for b in range(4):
                            if qidx >= 2:
                                pq = qidx - 2
                                E.wait(t, ("bev", pq // 64, (pq % 64) // 4,
                                           pq % 4))
                            for j in range(4):
                                kc = b * 4 + j
                                ins = t.transpose(
                                    pst[qidx % 2][:, j * 128:(j + 1) * 128],
                                    bdeq[:, (wl % 2) * 2048 + kc * 128:
                                         (wl % 2) * 2048 + kc * 128 + 128],
                                    idb[:])
                            E.inc(ins, ("btr", q, wl, b))
                            qidx += 1
                    for tt in range(NT):
                        E.wait(t, ("bht", q, tt))
                        if tt == 0:
                            E.wait(t, ("bev", q, 15, 3))
                        for m in range(4):
                            for cn in range(4):
                                if gl >= 2:
                                    E.wait(t, bank_rel[gl - 2])
                                for kc in range(KCH):
                                    ins = t.matmul(
                                        psd[gl % 2][:],
                                        hti[:, (tt % 2) * KCH * TT + kc * TT
                                            + m * 128:(tt % 2) * KCH * TT
                                            + kc * TT + m * 128 + 128],
                                        dtq[:, kc * D + cn * 512:
                                            kc * D + cn * 512 + 512],
                                        start=(kc == 0),
                                        stop=(kc == KCH - 1))
                                E.inc(ins, ("bmm", q, tt, m, cn))
                                gl += 1

            @blk.scalar
            def _(s):
                qidx = 0
                for q in range(NQ):
                    for wl in range(16):
                        cc = wl
                        for b in range(4):
                            E.wait(s, ("btr", q, wl, b))
                            if wl == 0 and b == 0 and q >= 1:
                                E.wait(s, ("bmm", q - 1, NT - 1, 3, 3))
                            E.inc(s.copy(
                                dtq[:].rearrange(
                                    "p (ic c) -> p ic c", ic=KCH
                                )[:, b * 4:(b + 1) * 4,
                                  cc * 128:cc * 128 + 128],
                                pst[qidx % 2][:].rearrange(
                                    "p (ic c) -> p ic c", ic=4)),
                                ("bev", q, wl, b))
                            qidx += 1


def emit_proj_b(p):
    nc = p.nc
    E = EvSys(nc, "p5")
    passes = [("Wq_p", "qb"), ("Wk_p", "kb"), ("Wv", "vb")]

    for i in range(3):
        E.define(("c", i), "c", 16)
    for pi in range(3):
        for lc in range(16):
            E.define(("wld", pi, lc), f"wl{lc % 2}", 16)
        for qn in range(64):
            E.define(("wtr", pi, qn), "tw", 1)
            E.define(("wev", pi, qn), "sw", 1)
        for tt in range(NT):
            E.define(("xt", pi, tt), f"xt{tt % 2}", 16)
        if pi < 2:
            for tt in range(NT):
                E.define(("cs", pi, tt), "vc", 1)
                for h in range(16):
                    E.define(("mm", pi, h, tt), "t", 1)
                    E.define(("rope", pi, h, tt), "vr", 1)
                    E.define(("st", pi, h, tt), f"o{(h + tt * 16) % 4}", 16)
        else:
            for tt in range(NT):
                for tm in range(4):
                    for cn in range(4):
                        E.define(("vmm", tt, tm, cn), "t", 1)
                        E.define(("vev", tt, tm, cn), "sv", 1)
                    E.define(("vst", tt, tm), f"vo{(tt * 4 + tm) % 2}", 16)

    import contextlib
    with contextlib.ExitStack() as _st:
        en = _st.enter_context
        idb = en(nc.sbuf_tensor("p5_idb", [128, 128], BF16))
        cosb = en(nc.sbuf_tensor("p5_cos", [64, L], F32))
        sinb = en(nc.sbuf_tensor("p5_sin", [64, L], F32))
        wld_b = en(nc.sbuf_tensor("p5_wld", [128, 2 * D], BF16))
        wT = en(nc.sbuf_tensor("p5_wT", [128, 16 * 16 * 128], BF16))
        xnt = en(nc.sbuf_tensor("p5_xnt", [128, 2 * 16 * TT], BF16))
        Ct = en(nc.sbuf_tensor("p5_C", [128, TT], F32))
        St = en(nc.sbuf_tensor("p5_S", [128, TT], F32))
        ra = en(nc.sbuf_tensor("p5_ra", [128, TT], F32))
        rc = en(nc.sbuf_tensor("p5_rc", [128, TT], F32))
        outst = en(nc.sbuf_tensor("p5_outst", [128, 4 * TT], BF16))
        vstage = en(nc.sbuf_tensor("p5_vstage", [128, 2 * D], BF16))
        psm = [en(nc.psum_tensor(f"p5_ps{i}", [128, 512], F32))
               for i in range(2)]
        pst = [en(nc.psum_tensor(f"p5_pst{i}", [128, 512], BF16))
               for i in range(2)]

        xnr = p.d["xn3"].rearrange("(kc q) t -> q kc t", q=128)
        xnt3 = xnt[:].rearrange("q (b kc t) -> q b kc t", b=2, kc=16)
        wT3 = wT[:].rearrange("q (kc lc c) -> q kc lc c", kc=16, lc=16)
        sFv = Sem(nc, "p5fv")
        rel_q = {}

        with nc.Block() as blk:
            @blk.sync
            def _(sync):
                E.inc(sync.dma_start(idb[:], p.d["c_id_bf"][:]), ("c", 0))
                E.inc(sync.dma_start(cosb[:], p.d["c_cos"][:]), ("c", 1))
                E.inc(sync.dma_start(sinb[:], p.d["c_sin"][:]), ("c", 2))
                for pi in range(3):
                    for tt in range(NT):
                        u = pi * NT + tt
                        if u >= 2:
                            pp, ptt = divmod(u - 2, NT)
                            if pp < 2:
                                E.wait(sync, ("mm", pp, 15, ptt))
                            else:
                                E.wait(sync, ("vmm", ptt, 3, 3))
                        E.inc(sync.dma_start(
                            xnt3[:, tt % 2, :, :],
                            xnr[:, :, tt * TT:(tt + 1) * TT]), ("xt", pi, tt))

            @blk.gpsimd
            def _(g):
                def wload(pi, lc):
                    u = pi * 16 + lc
                    if u >= 2:
                        pp, plc = divmod(u - 2, 16)
                        E.wait(g, ("wtr", pp, plc * 4 + 3))
                    wsrc = p.d[passes[pi][0]]
                    E.inc(g.dma_start(
                        wld_b[:].rearrange("q (b d) -> q b d", b=2)[:, lc % 2, :],
                        wsrc[lc * 128:(lc + 1) * 128, :]), ("wld", pi, lc))

                for lc in range(16):
                    wload(0, lc)
                for pi in range(2):
                    for tt in range(NT):
                        for h in range(16):
                            n = tt * 16 + h
                            E.wait(g, ("rope", pi, h, tt))
                            dst = p.d[passes[pi][1]]
                            slot = (h + tt * 16) % 4
                            E.inc(g.dma_start(
                                dst[h * 128:(h + 1) * 128,
                                    tt * TT:(tt + 1) * TT],
                                outst[:, slot * TT:(slot + 1) * TT]),
                                ("st", pi, h, tt))
                    for lc in range(16):
                        wload(pi + 1, lc)
                for tt in range(NT):
                    for tm in range(4):
                        E.wait(g, ("vev", tt, tm, 3))
                        sl = (tt * 4 + tm) % 2
                        E.inc(g.dma_start(
                            p.d["vb"][(tt * 4 + tm) * 128:
                                      (tt * 4 + tm) * 128 + 128, :],
                            vstage[:, sl * D:(sl + 1) * D]), ("vst", tt, tm))
                for i in range(4):
                    g.wait_ge(E.sems[f"o{i}"].h, E.cnt[f"o{i}"])
                g.wait_ge(E.sems["vo0"].h, E.cnt["vo0"])
                g.wait_ge(E.sems["vo1"].h, E.cnt["vo1"])

            @blk.tensor
            def _(t):
                t.wait_ge(E.sems["c"].h, E.cnt["c"])
                gq = [0]

                def quads(pi, lc):
                    E.wait(t, ("wld", pi, lc))
                    for b in range(4):
                        qn = lc * 4 + b
                        if gq[0] >= 2:
                            pgq = gq[0] - 2
                            E.wait(t, ("wev", pgq // 64, pgq % 64))
                        for j in range(4):
                            kc = b * 4 + j
                            ins = t.transpose(
                                pst[gq[0] % 2][:, j * 128:(j + 1) * 128],
                                wld_b[:, (lc % 2) * D + kc * 128:
                                      (lc % 2) * D + kc * 128 + 128],
                                idb[:])
                        E.inc(ins, ("wtr", pi, qn))
                        gq[0] += 1

                for lc in range(16):
                    quads(0, lc)
                for pi in range(3):
                    if pi >= 1:
                        for lc in range(16):
                            quads(pi, lc)
                    for tt in range(NT):
                        E.wait(t, ("xt", pi, tt))
                        if tt == 0:
                            E.wait(t, ("wev", pi, 63))
                        if pi < 2:
                            for h in range(16):
                                n = tt * 16 + h
                                bank = n % 2
                                gidx = (pi * NT + tt) * 16 + h
                                if gidx >= 2:
                                    rk = rel_q.get(gidx - 2)
                                    if rk is not None:
                                        E.wait(t, rk)
                                for kc in range(16):
                                    ins = t.matmul(
                                        psm[bank][:],
                                        wT3[:, kc, h, :],
                                        xnt[:, (tt % 2) * 16 * TT + kc * TT:
                                            (tt % 2) * 16 * TT + (kc + 1) * TT],
                                        start=(kc == 0), stop=(kc == 15))
                                E.inc(ins, ("mm", pi, h, tt))
                                rel_q[gidx] = ("rope", pi, h, tt)
                        else:
                            for tm in range(4):
                                for cn in range(4):
                                    gidx = 256 + (tt * 4 + tm) * 4 + cn
                                    if gidx >= 2:
                                        rk = rel_q.get(gidx - 2)
                                        if rk is not None:
                                            E.wait(t, rk)
                                    for kc in range(16):
                                        ins = t.matmul(
                                            psm[gidx % 2][:],
                                            xnt[:, (tt % 2) * 16 * TT + kc * TT
                                                + tm * 128:(tt % 2) * 16 * TT
                                                + kc * TT + tm * 128 + 128],
                                            wT3[:, kc, :, :].rearrange(
                                                "q lc c -> q (lc c)")[
                                                :, cn * 512:(cn + 1) * 512],
                                            start=(kc == 0), stop=(kc == 15))
                                    E.inc(ins, ("vmm", tt, tm, cn))
                                    rel_q[gidx] = ("vev", tt, tm, cn)

            @blk.vector
            def _(v):
                v.wait_ge(E.sems["c"].h, E.cnt["c"])
                prev_rope = [None]
                for pi in range(2):
                    for tt in range(NT):
                        if tt >= 1:
                            E.wait(v, ("rope", pi, 15, tt - 1))
                        elif pi == 1:
                            E.wait(v, ("rope", 0, 15, NT - 1))
                        cslice = cosb[:, tt * TT:(tt + 1) * TT]
                        sslice = sinb[:, tt * TT:(tt + 1) * TT]
                        fence(v, sFv, v.tensor_copy(Ct[0:64, :], cslice))
                        fence(v, sFv, v.tensor_copy(Ct[64:128, :], cslice))
                        fence(v, sFv, v.tensor_scalar_mul(St[0:64, :],
                                                          sslice, -1.0))
                        E.inc(v.tensor_copy(St[64:128, :], sslice),
                              ("cs", pi, tt))
                        v.wait_ge(E.sems["vc"].h, E.val[("cs", pi, tt)][1])
                        for h in range(16):
                            n = tt * 16 + h
                            bank = n % 2
                            E.wait(v, ("mm", pi, h, tt))
                            slot = (h + tt * 16) % 4
                            if (pi * NT + tt) * 16 + h >= 4:
                                gidx = (pi * NT + tt) * 16 + h - 4
                                pp, r = divmod(gidx, NT * 16)
                                ptt, ph = divmod(r, 16)
                                E.wait(v, ("st", pp, ph, ptt))
                            ps = psm[bank]
                            if prev_rope[0] is not None:
                                E.wait(v, prev_rope[0])
                            prev_rope[0] = ("rope", pi, h, tt)
                            fence(v, sFv, v.tensor_mul(ra[:], ps[:], Ct[:]))
                            fence(v, sFv, v.tensor_mul(rc[0:64, :],
                                                       ps[64:128, :],
                                                       St[0:64, :]))
                            fence(v, sFv, v.tensor_mul(rc[64:128, :],
                                                       ps[0:64, :],
                                                       St[64:128, :]))
                            E.inc(v.tensor_add(
                                outst[:, slot * TT:(slot + 1) * TT],
                                ra[:], rc[:]), ("rope", pi, h, tt))

            @blk.scalar
            def _(s):
                gq = [0]
                for pi in range(3):
                    for lc in range(16):
                        for b in range(4):
                            qn = lc * 4 + b
                            E.wait(s, ("wtr", pi, qn))
                            if qn == 0 and pi >= 1:
                                if pi == 1:
                                    E.wait(s, ("mm", 0, 15, NT - 1))
                                else:
                                    E.wait(s, ("mm", 1, 15, NT - 1))
                            E.inc(s.copy(
                                wT3[:, b * 4:(b + 1) * 4, lc, :],
                                pst[gq[0] % 2][:].rearrange(
                                    "q (j c) -> q j c", j=4)),
                                ("wev", pi, qn))
                            gq[0] += 1
                    if pi == 2:
                        for tt in range(NT):
                            for tm in range(4):
                                sl = (tt * 4 + tm) % 2
                                for cn in range(4):
                                    E.wait(s, ("vmm", tt, tm, cn))
                                    if cn == 0 and tt * 4 + tm >= 2:
                                        pst_ = tt * 4 + tm - 2
                                        E.wait(s, ("vst", pst_ // 4, pst_ % 4))
                                    E.inc(s.copy(
                                        vstage[:, sl * D + cn * 512:
                                               sl * D + cn * 512 + 512],
                                        psm[(256 + (tt * 4 + tm) * 4 + cn)
                                            % 2][:]), ("vev", tt, tm, cn))

    return


def emit_wattn(p):
    nc = p.nc
    E = EvSys(nc, "p6")
    NCHW = 16
    SCL = DK ** -0.5

    for i in range(5):
        E.define(("c", i), "c", 16)
    for h in range(HK):
        for ch in range(NCHW):
            u = h * NCHW + ch
            E.define(("lq", h, ch), f"lq{u % 2}", 16)
            E.define(("lk", h, ch), f"lk{u % 2}", 16)
            E.define(("lv", h, ch), f"lv{u % 2}", 16)
            if ch == 0:
                E.define(("lk0", h), f"lk{u % 2}", 16)
                E.define(("lv0", h), f"lv{u % 2}", 16)
            for hf in range(2):
                E.define(("smm", h, ch, hf), "t", 1)
                E.define(("msk", h, ch, hf), "vm", 1)
                E.define(("rmx", h, ch, hf), "vm", 1)
                E.define(("exp", h, ch, hf), "se", 1)
            for hf in range(2):
                E.define(("att", h, ch, hf), "vm", 1)
            for qd in range(2):
                E.define(("atr", h, ch, qd), "t", 1)
                E.define(("aev", h, ch, qd), "sa", 1)
            E.define(("omm", h, ch), "t", 1)
            E.define(("oev", h, ch), "so", 1)
            E.define(("ost", h, ch), f"o{u % 2}", 16)

    import contextlib
    with contextlib.ExitStack() as _st:
        en = _st.enter_context
        idb = en(nc.sbuf_tensor("p6_idb", [128, 128], BF16))
        wm = [en(nc.sbuf_tensor(f"p6_wm{i}", [128, 512], F32)) for i in range(4)]
        qt = en(nc.sbuf_tensor("p6_qt", [128, 2 * 512], BF16))
        kt = en(nc.sbuf_tensor("p6_kt", [128, 2 * 512], BF16))
        vt = en(nc.sbuf_tensor("p6_vt", [128, 2 * 512], BF16))
        sc = en(nc.sbuf_tensor("p6_sc", [128, 2 * 2 * 512], F32))
        et = en(nc.sbuf_tensor("p6_et", [128, 2 * 2 * 512], BF16))
        at = en(nc.sbuf_tensor("p6_at", [128, 2 * 2 * 512], BF16))
        atT = en(nc.sbuf_tensor("p6_atT", [128, 2 * 4 * 256], BF16))
        ost = en(nc.sbuf_tensor("p6_ost", [128, 2 * 256], BF16))
        nmx = [en(nc.sbuf_tensor(f"p6_nmx{i}", [128, 1], F32)) for i in range(2)]
        sm = [en(nc.sbuf_tensor(f"p6_sm{i}", [128, 1], F32)) for i in range(2)]
        rc = en(nc.sbuf_tensor("p6_rc", [128, 2], F32))
        pss = [en(nc.psum_tensor(f"p6_pss{i}", [128, 512], F32))
               for i in range(2)]
        pst = [en(nc.psum_tensor(f"p6_pst{i}", [128, 512], BF16))
               for i in range(2)]
        pso = [en(nc.psum_tensor(f"p6_pso{i}", [128, 256], F32))
               for i in range(2)]

        vre = p.d["vb"].rearrange("(jb q) c -> q jb c", q=128)
        sc4 = sc[:].rearrange("q (b hf c) -> q b hf c", b=2, hf=2)
        et4 = et[:].rearrange("q (b hf c) -> q b hf c", b=2, hf=2)
        at5 = at[:].rearrange("q (b hf jb c) -> q b hf jb c", b=2, hf=2, jb=4)
        at4 = at[:].rearrange("q (b hf c) -> q b hf c", b=2, hf=2)
        atT4 = atT[:].rearrange("q (b jb i) -> q b jb i", b=2, jb=4)
        vt4 = vt[:].rearrange("q (b jb c) -> q b jb c", b=2, jb=4)
        sFv = Sem(nc, "p6fv")

        with nc.Block() as blk:
            @blk.sync
            def _(sync):
                E.inc(sync.dma_start(idb[:], p.d["c_id_bf"][:]), ("c", 0))
                for i, nm in enumerate(["c_wmask10", "c_wmask11",
                                        "c_wmask00", "c_wmask01"]):
                    E.inc(sync.dma_start(wm[i][:], p.d[nm][:]), ("c", i + 1))
                for h in range(HK):
                    for ch in range(NCHW):
                        u = h * NCHW + ch
                        sl = u % 2
                        c0 = 0 if ch == 0 else (ch - 1) * 256
                        if u >= 2:
                            E.wait(sync, ("omm", *divmod(u - 2, NCHW)))
                        E.inc(sync.dma_start(
                            qt[:, sl * 512:sl * 512 + 256],
                            p.d["qb"][h * 128:(h + 1) * 128,
                                      ch * 256:(ch + 1) * 256]), ("lq", h, ch))
                        if ch == 0:
                            E.inc(sync.dma_start(
                                kt[:, sl * 512 + 256:(sl + 1) * 512],
                                p.d["kb"][h * 128:(h + 1) * 128, 0:256]),
                                ("lk", h, ch))
                            E.inc(sync.dma_start(
                                kt[:, sl * 512:sl * 512 + 256],
                                p.d["kb"][h * 128:(h + 1) * 128, 0:256]),
                                ("lk0", h))
                            E.inc(sync.dma_start(
                                vt4[:, sl, 2:4, :],
                                vre[:, 0:2, h * 128:(h + 1) * 128]),
                                ("lv", h, ch))
                            E.inc(sync.dma_start(
                                vt4[:, sl, 0:2, :],
                                vre[:, 0:2, h * 128:(h + 1) * 128]),
                                ("lv0", h))
                        else:
                            E.inc(sync.dma_start(
                                kt[:, sl * 512:(sl + 1) * 512],
                                p.d["kb"][h * 128:(h + 1) * 128, c0:c0 + 512]),
                                ("lk", h, ch))
                            E.inc(sync.dma_start(
                                vt4[:, sl, :, :],
                                vre[:, c0 // 128:c0 // 128 + 4,
                                    h * 128:(h + 1) * 128]), ("lv", h, ch))

            @blk.tensor
            def _(t):
                t.wait_ge(E.sems["c"].h, E.cnt["c"])
                gq = [0]
                for h in range(HK):
                    for ch in range(NCHW):
                        u = h * NCHW + ch
                        sl = u % 2
                        E.wait(t, ("lq", h, ch))
                        if ch == 0:
                            E.wait(t, ("lk0", h))
                        else:
                            E.wait(t, ("lk", h, ch))
                        for hf in range(2):
                            if u >= 2:
                                E.wait(t, ("msk", *divmod(u - 2, NCHW), hf))
                            E.inc(t.matmul(
                                pss[hf][:],
                                qt[:, sl * 512 + hf * 128:
                                   sl * 512 + hf * 128 + 128],
                                kt[:, sl * 512:(sl + 1) * 512],
                                start=True, stop=True), ("smm", h, ch, hf))
                        for qd in range(2):
                            if qd == 0:
                                E.wait(t, ("att", h, ch, 0))
                                E.wait(t, ("att", h, ch, 1))
                            if gq[0] >= 2:
                                pgq = gq[0] - 2
                                E.wait(t, ("aev", (pgq // 2) // NCHW,
                                           (pgq // 2) % NCHW, pgq % 2))
                            for j in range(2):
                                for hf in range(2):
                                    ins = t.transpose(
                                        pst[gq[0] % 2][:, (j * 2 + hf) * 128:
                                            (j * 2 + hf) * 128 + 128],
                                        at5[:, sl, hf, qd * 2 + j, :],
                                        idb[:])
                            E.inc(ins, ("atr", h, ch, qd))
                            gq[0] += 1
                        if ch == 0:
                            E.wait(t, ("lv0", h))
                        else:
                            E.wait(t, ("lv", h, ch))
                        E.wait(t, ("aev", h, ch, 0))
                        E.wait(t, ("aev", h, ch, 1))
                        if u >= 2:
                            E.wait(t, ("oev", *divmod(u - 2, NCHW)))
                        for jb in range(4):
                            ins = t.matmul(
                                pso[sl][:],
                                vt4[:, sl, jb, :],
                                atT4[:, sl, jb, :],
                                start=(jb == 0), stop=(jb == 3))
                        E.inc(ins, ("omm", h, ch))

            @blk.vector
            def _(v):
                v.wait_ge(E.sems["c"].h, E.cnt["c"])
                for h in range(HK):
                    for ch in range(NCHW):
                        u = h * NCHW + ch
                        sl = u % 2
                        for hf in range(2):
                            E.wait(v, ("smm", h, ch, hf))
                            if u >= 2:
                                E.wait(v, ("exp", *divmod(u - 2, NCHW), hf))
                            mask = wm[(0 if ch > 0 else 2) + hf]
                            E.inc(v.scalar_tensor_tensor(
                                sc4[:, sl, hf, :], pss[hf][:], SCL, mask[:],
                                ALU.mult, ALU.add), ("msk", h, ch, hf))
                            v.wait_ge(E.sems["vm"].h,
                                      E.val[("msk", h, ch, hf)][1])
                            if u >= 1:
                                E.wait(v, ("exp", *divmod(u - 1, NCHW), hf))
                            E.inc(v.reduce_max(
                                nmx[hf][:], sc4[:, sl, hf, :],
                                axis=AX, negate=True), ("rmx", h, ch, hf))
                        for hf in range(2):
                            E.wait(v, ("exp", h, ch, hf))
                            fence(v, sFv, v.reciprocal(rc[:, hf:hf + 1],
                                                       sm[hf][:]))
                            if hf == 0 and u >= 2:
                                E.wait(v, ("atr", *divmod(u - 2, NCHW), 1))
                            E.inc(v.tensor_scalar_mul(
                                at4[:, sl, hf, :], et4[:, sl, hf, :],
                                rc[:, hf:hf + 1]), ("att", h, ch, hf))

            @blk.scalar
            def _(s):
                gq = [0]
                for h in range(HK):
                    for ch in range(NCHW):
                        u = h * NCHW + ch
                        sl = u % 2
                        for hf in range(2):
                            E.wait(s, ("rmx", h, ch, hf))
                            if u >= 1:
                                E.wait(s, ("att", *divmod(u - 1, NCHW), hf))
                            E.inc(s.activation(
                                et4[:, sl, hf, :], sc4[:, sl, hf, :],
                                AF.Exp, bias=nmx[hf][:],
                                accum_out=sm[hf][:]), ("exp", h, ch, hf))
                        for qd in range(2):
                            E.wait(s, ("atr", h, ch, qd))
                            if qd == 0 and u >= 2:
                                E.wait(s, ("omm", *divmod(u - 2, NCHW)))
                            E.inc(s.copy(
                                atT4[:, sl, qd * 2:(qd + 1) * 2, :].rearrange(
                                    "q jb i -> q (jb i)"),
                                pst[gq[0] % 2][:]), ("aev", h, ch, qd))
                            gq[0] += 1
                        E.wait(s, ("omm", h, ch))
                        if u >= 2:
                            E.wait(s, ("ost", *divmod(u - 2, NCHW)))
                        E.inc(s.copy(ost[:, sl * 256:(sl + 1) * 256],
                                     pso[sl][:]), ("oev", h, ch))

            @blk.gpsimd
            def _(g):
                for h in range(HK):
                    for ch in range(NCHW):
                        u = h * NCHW + ch
                        sl = u % 2
                        E.wait(g, ("oev", h, ch))
                        E.inc(g.dma_start(
                            p.d["ob"][h * 128:(h + 1) * 128,
                                      ch * 256:(ch + 1) * 256],
                            ost[:, sl * 256:(sl + 1) * 256]), ("ost", h, ch))
                g.wait_ge(E.sems["o0"].h, E.cnt["o0"])
                g.wait_ge(E.sems["o1"].h, E.cnt["o1"])


def emit_wo(p):
    nc = p.nc
    E = EvSys(nc, "p7")
    E.define(("c", 0), "c", 16)
    for lc in range(16):
        E.define(("wld", lc), f"wl{lc % 2}", 16)
    for qn in range(64):
        E.define(("wtr", qn), "tw", 1)
        E.define(("wev", qn), "sw", 1)
    for tt in range(NT):
        E.define(("xt", tt), f"xt{tt % 2}", 16)
        for tm in range(4):
            for cn in range(4):
                E.define(("mm", tt, tm, cn), "t", 1)
                E.define(("ev", tt, tm, cn), "sv", 1)
            E.define(("st", tt, tm), f"vo{(tt * 4 + tm) % 2}", 16)

    import contextlib
    with contextlib.ExitStack() as _st:
        en = _st.enter_context
        idb = en(nc.sbuf_tensor("p7_idb", [128, 128], BF16))
        wld_b = en(nc.sbuf_tensor("p7_wld", [128, 2 * D], BF16))
        wT = en(nc.sbuf_tensor("p7_wT", [128, 16 * 16 * 128], BF16))
        obt = en(nc.sbuf_tensor("p7_obt", [128, 2 * 16 * TT], BF16))
        vstage = en(nc.sbuf_tensor("p7_vstage", [128, 2 * D], F32))
        psm = [en(nc.psum_tensor(f"p7_ps{i}", [128, 512], F32))
               for i in range(2)]
        pst = [en(nc.psum_tensor(f"p7_pst{i}", [128, 512], BF16))
               for i in range(2)]

        obr = p.d["ob"].rearrange("(kc q) t -> q kc t", q=128)
        obt3 = obt[:].rearrange("q (b kc t) -> q b kc t", b=2, kc=16)
        wT3 = wT[:].rearrange("q (kc lc c) -> q kc lc c", kc=16, lc=16)
        rel_q = {}

        with nc.Block() as blk:
            @blk.sync
            def _(sync):
                E.inc(sync.dma_start(idb[:], p.d["c_id_bf"][:]), ("c", 0))
                for tt in range(NT):
                    if tt >= 2:
                        E.wait(sync, ("mm", tt - 2, 3, 3))
                    E.inc(sync.dma_start(
                        obt3[:, tt % 2, :, :],
                        obr[:, :, tt * TT:(tt + 1) * TT]), ("xt", tt))

            @blk.gpsimd
            def _(g):
                for lc in range(16):
                    if lc >= 2:
                        E.wait(g, ("wtr", (lc - 2) * 4 + 3))
                    E.inc(g.dma_start(
                        wld_b[:].rearrange("q (b d) -> q b d", b=2)[:, lc % 2, :],
                        p.d["Wo"][lc * 128:(lc + 1) * 128, :]), ("wld", lc))
                for tt in range(NT):
                    for tm in range(4):
                        E.wait(g, ("ev", tt, tm, 3))
                        sl = (tt * 4 + tm) % 2
                        E.inc(g.dma_start(
                            p.d["sub_out"][(tt * 4 + tm) * 128:
                                           (tt * 4 + tm) * 128 + 128, :],
                            vstage[:, sl * D:(sl + 1) * D]), ("st", tt, tm))
                g.wait_ge(E.sems["vo0"].h, E.cnt["vo0"])
                g.wait_ge(E.sems["vo1"].h, E.cnt["vo1"])

            @blk.tensor
            def _(t):
                t.wait_ge(E.sems["c"].h, E.cnt["c"])
                gq = [0]
                for lc in range(16):
                    E.wait(t, ("wld", lc))
                    for b in range(4):
                        qn = lc * 4 + b
                        if gq[0] >= 2:
                            E.wait(t, ("wev", gq[0] - 2))
                        for j in range(4):
                            kc = b * 4 + j
                            ins = t.transpose(
                                pst[gq[0] % 2][:, j * 128:(j + 1) * 128],
                                wld_b[:, (lc % 2) * D + kc * 128:
                                      (lc % 2) * D + kc * 128 + 128],
                                idb[:])
                        E.inc(ins, ("wtr", qn))
                        gq[0] += 1
                for tt in range(NT):
                    E.wait(t, ("xt", tt))
                    if tt == 0:
                        E.wait(t, ("wev", 63))
                    for tm in range(4):
                        for cn in range(4):
                            gidx = (tt * 4 + tm) * 4 + cn
                            if gidx >= 2:
                                rk = rel_q.get(gidx - 2)
                                if rk is not None:
                                    E.wait(t, rk)
                            for kc in range(16):
                                ins = t.matmul(
                                    psm[gidx % 2][:],
                                    obt[:, (tt % 2) * 16 * TT + kc * TT
                                        + tm * 128:(tt % 2) * 16 * TT
                                        + kc * TT + tm * 128 + 128],
                                    wT3[:, kc, :, :].rearrange(
                                        "q lc c -> q (lc c)")[
                                        :, cn * 512:(cn + 1) * 512],
                                    start=(kc == 0), stop=(kc == 15))
                            E.inc(ins, ("mm", tt, tm, cn))
                            rel_q[gidx] = ("ev", tt, tm, cn)

            @blk.scalar
            def _(s):
                gq = [0]
                for lc in range(16):
                    for b in range(4):
                        qn = lc * 4 + b
                        E.wait(s, ("wtr", qn))
                        E.inc(s.copy(
                            wT3[:, b * 4:(b + 1) * 4, lc, :],
                            pst[gq[0] % 2][:].rearrange(
                                "q (j c) -> q j c", j=4)), ("wev", qn))
                        gq[0] += 1
                for tt in range(NT):
                    for tm in range(4):
                        sl = (tt * 4 + tm) % 2
                        for cn in range(4):
                            E.wait(s, ("mm", tt, tm, cn))
                            if cn == 0 and tt * 4 + tm >= 2:
                                pst_ = tt * 4 + tm - 2
                                E.wait(s, ("st", pst_ // 4, pst_ % 4))
                            E.inc(s.copy(
                                vstage[:, sl * D + cn * 512:
                                       sl * D + cn * 512 + 512],
                                psm[((tt * 4 + tm) * 4 + cn) % 2][:]),
                                ("ev", tt, tm, cn))


def emit_scan(p):
    nc = p.nc
    E = EvSys(nc, "p2")
    NG4, NLV = 8, 6

    def pg(c, gq):
        if gq > 0:
            return (c, gq - 1)
        if c > 0:
            return (c - 1, NG4 - 1)
        return None

    def wif(eng, key):
        if key is not None and key in E.val:
            E.wait(eng, key)

    for i in range(8):
        E.define(("c", i), "c", 16)

    def def_mach(c):
        E.define(("bmm", c), "t", 1)
        E.define(("ebtr", c), "t", 1)
        E.define(("ebbc", c), "t", 1)
        E.define(("bsb", c), "v", 1)
        E.define(("ebcv", c), "v", 1)
        E.define(("ebc1", c), "s", 1)
        E.define(("ebrow", c), "s", 1)
        E.define(("bw", c), f"bw{c % 2}", 16)
        E.define(("brow", c), f"br{c % 2}", 16)
        E.define(("bgrp", c), f"bg{c % 2}", 16)
        E.define(("btg", c), f"bt{c % 2}", 16)

    def def_group(c, gq):
        E.define(("t1", c, gq), "t", 1)
        E.define(("r2", c, gq), "t", 1)
        E.define(("nbm", c, gq), "t", 1)
        E.define(("bcr", c, gq), "t", 1)
        E.define(("eb2m", c, gq), "t", 1)
        for i4 in range(4):
            E.define(("kk", c, gq, i4), "t", 1)
        for i4 in range(4):
            E.define(("qk", c, gq, i4), "t", 1)
        for i4 in range(4):
            E.define(("ws", c, gq, i4), "t", 1)
        E.define(("utr", c, gq), "t", 1)
        E.define(("x1tr", c, gq), "t", 1)
        for i4 in range(4):
            E.define(("g0m", c, gq, i4), "t", 1)
        for lvl in range(1, NLV + 1):
            E.define(("sql", c, gq, lvl), "t", 1)
            for i4 in range(4):
                E.define(("gmm", c, gq, lvl, i4), "t", 1)
        for i4 in range(4):
            E.define(("omm", c, gq, i4), "t", 1)
        E.define(("ktr", c, gq), "t", 1)
        for i4 in range(4):
            E.define(("smm", c, gq, i4), "t", 1)
        E.define(("mskv", c, gq), "v", 1)
        E.define(("negb", c, gq), "v", 1)
        E.define(("einc", c, gq), "v", 1)
        E.define(("row3", c, gq), "v", 1)
        E.define(("n1a", c, gq), "v", 1)
        E.define(("n1b", c, gq), "v", 1)
        E.define(("mtv", c, gq), "v", 1)
        for i4 in range(4):
            E.define(("ufm", c, gq, i4), "v", 1)
        E.define(("g0", c, gq), "v", 1)
        for lvl in range(1, NLV + 1):
            E.define(("gup", c, gq, lvl), "v", 1)
        for i4 in range(4):
            E.define(("sup", c, gq, i4), "v", 1)
        E.define(("eb", c, gq), "s", 1)
        E.define(("estr", c, gq), "s", 1)
        E.define(("eb2", c, gq), "s", 1)
        E.define(("utm", c, gq), "s", 1)
        E.define(("xu0", c, gq), "s", 1)
        for lvl in range(1, NLV + 1):
            E.define(("xte", c, gq, lvl), "s", 1)
            if lvl < NLV:
                E.define(("xue", c, gq, lvl), "s", 1)
        E.define(("oev", c, gq), "s", 1)
        E.define(("ktm", c, gq), "s", 1)
        E.define(("sbf", c, gq), "s", 1)
        for i4 in range(4):
            E.define(("wfm", c, gq, i4), "g", 1)
        for i4 in range(4):
            E.define(("qtm", c, gq, i4), "g", 1)
        for i4 in range(4):
            E.define(("dlt", c, gq, i4), "g", 1)
        for i4 in range(4):
            E.define(("ktl", c, gq, i4), "g", 1)
        for i4 in range(4):
            E.define(("ost", c, gq, i4), f"o{gq % 2}{i4 % 2}", 16)

    def_mach(0)
    def_mach(1)
    for c in range(NCH):
        if c % SEGC == 0:
            E.define(("sz", c), "v", 1)
        for gq in range(NG4):
            def_group(c, gq)
        if c + 2 < NCH:
            def_mach(c + 2)

    import contextlib
    with contextlib.ExitStack() as _st:
        en = _st.enter_context
        U128 = en(nc.sbuf_tensor("p2_U", [128, 128], F32))
        idf = en(nc.sbuf_tensor("p2_idf", [128, 128], F32))
        idb = en(nc.sbuf_tensor("p2_idb", [128, 128], BF16))
        m4 = en(nc.sbuf_tensor("p2_m4", [128, 512], F32))
        i4c = en(nc.sbuf_tensor("p2_i4", [128, 512], F32))
        sel4 = en(nc.sbuf_tensor("p2_sel4", [4, 512], F32))
        nsel4 = en(nc.sbuf_tensor("p2_nsel4", [4, 512], F32))
        ones_r = en(nc.sbuf_tensor("p2_ones", [1, 128], F32))
        qkb = en(nc.sbuf_tensor("p2_qkb", [128, 2 * 32 * 512], BF16))
        vbuf = en(nc.sbuf_tensor("p2_vbuf", [128, 32 * 512], BF16))
        S_sb = en(nc.sbuf_tensor("p2_Ssb", [128, 32 * 128], F32))
        S_bf = en(nc.sbuf_tensor("p2_Sbf", [128, 32 * 128], BF16))
        b_sb = [en(nc.sbuf_tensor(f"p2_bsb{i}", [32, 128], F32))
                for i in range(2)]
        b_row = [en(nc.sbuf_tensor("p2_brow", [1, 4096], F32))] * 2
        b_grp = [en(nc.sbuf_tensor("p2_bgrp", [4, 1024], F32))] * 2
        bt_grp = [en(nc.sbuf_tensor("p2_btgrp", [4, 1024], F32))] * 2
        ebc32 = [en(nc.sbuf_tensor(f"p2_ebc32{i}", [32, 1], F32))
                 for i in range(2)]
        ebcrow = [en(nc.sbuf_tensor(f"p2_ebcrow{i}", [1, 32], F32))
                  for i in range(2)]
        ebC_bc = [en(nc.sbuf_tensor(f"p2_ebcbc{i}", [128, 32], F32))
                  for i in range(2)]
        EBt = en(nc.sbuf_tensor("p2_EB", [128, 512], F32))
        Estr = en(nc.sbuf_tensor("p2_Estr", [128, 512], F32))
        Einc = en(nc.sbuf_tensor("p2_Einc", [128, 512], F32))
        EB2t = en(nc.sbuf_tensor("p2_EB2", [128, 512], F32))
        negb = en(nc.sbuf_tensor("p2_negb", [128, 512], F32))
        mtmp = en(nc.sbuf_tensor("p2_mtmp", [128, 512], F32))
        ntmp = en(nc.sbuf_tensor("p2_ntmp", [128, 512], F32))
        row3 = en(nc.sbuf_tensor("p2_row3", [1, 512], F32))
        N1t = en(nc.sbuf_tensor("p2_N1t", [128, 512], BF16))
        Mt = en(nc.sbuf_tensor("p2_Mt", [128, 512], BF16))
        wfm_w = en(nc.sbuf_tensor("p2_wfm", [128, 512], BF16))
        u_fm_w = en(nc.sbuf_tensor("p2_ufmw", [128, 512], BF16))
        u_tm_w = en(nc.sbuf_tensor("p2_utmw", [128, 512], BF16))
        Xu0 = en(nc.sbuf_tensor("p2_Xu0", [128, 512], BF16))
        Xt_sb = en(nc.sbuf_tensor("p2_Xt", [128, 2 * 512], BF16))
        Xu_sb = en(nc.sbuf_tensor("p2_Xu", [128, 2 * 512], BF16))
        gam = en(nc.sbuf_tensor("p2_gam", [128, 512], BF16))
        dlt_w = en(nc.sbuf_tensor("p2_dltw", [128, 512], BF16))
        qt_w = en(nc.sbuf_tensor("p2_qtw", [128, 512], BF16))
        ktl_w = en(nc.sbuf_tensor("p2_ktlw", [128, 512], BF16))
        ktm_w = en(nc.sbuf_tensor("p2_ktmw", [128, 512], BF16))
        ostg = en(nc.sbuf_tensor("p2_ostg", [128, 2 * 512], BF16))
        ps_misc = en(nc.psum_tensor("p2_ps0", [128, 512], F32))
        ps_r2 = en(nc.psum_tensor("p2_ps1", [128, 512], F32))
        ps_kk = en(nc.psum_tensor("p2_ps2", [128, 512], F32))
        ps_qk = en(nc.psum_tensor("p2_ps3", [128, 512], F32))
        ps_ws = en(nc.psum_tensor("p2_ps4", [128, 512], F32))
        ps_s1 = en(nc.psum_tensor("p2_ps5", [128, 512], F32))
        ps_s2 = en(nc.psum_tensor("p2_ps6", [128, 512], F32))
        ps_tq = en(nc.psum_tensor("p2_ps7", [128, 512], BF16))

        bscr = p.d["bscr"]
        g_sb = p.sb["g_sb"]
        beta_sb = p.sb["beta_sb"]
        sQv = [Sem(nc, "p2qv0"), Sem(nc, "p2qv1")]
        sVv = Sem(nc, "p2vv")

        qre = p.d["qa"].rearrange("(kh q) t -> q kh t", q=128)
        kre = p.d["ka"].rearrange("(kh q) t -> q kh t", q=128)
        vre = p.d["va"].rearrange("(vh q) t -> q vh t", q=128)
        qk3 = qkb[:].rearrange("q (b kh two t) -> q b kh two t",
                               b=2, kh=16, two=2)
        vb3 = vbuf[:].rearrange("q (vh t) -> q vh t", vh=32)

        def KF(c, h):
            base = (c // 4 % 2) * 16384 + (h // 2) * 1024 + 512
            return qkb[:, base + (c % 4) * 128:base + (c % 4) * 128 + 128]

        def QF(c, h):
            base = (c // 4 % 2) * 16384 + (h // 2) * 1024
            return qkb[:, base + (c % 4) * 128:base + (c % 4) * 128 + 128]

        def VF(c, h):
            base = h * 512
            return vbuf[:, base + (c % 4) * 128:base + (c % 4) * 128 + 128]

        def s4(buf, i4, w=128):
            return buf[:, i4 * w:(i4 + 1) * w]

        with nc.Block() as blk:
            @blk.sync
            def _(sync):
                for i, (dst, nm) in enumerate([
                        (U128, "c_U128"), (idf, "c_id_f32"), (idb, "c_id_bf"),
                        (m4, "c_maskstrict4"), (i4c, "c_id01_4"),
                        (sel4, "c_sel4"), (nsel4, "c_negsel4"),
                        (ones_r, "c_ones_1x128_f32")]):
                    E.inc(sync.dma_start(dst[:], p.d[nm][:]), ("c", i))
                def qvload(tc):
                    b = tc % 2
                    if tc >= 2:
                        E.wait(sync, ("sbf", tc * 4 - 5, NG4 - 1))
                    for kh in range(16):
                        sync.dma_start(
                            qk3[:, b, kh, 0, :],
                            qre[:, kh, tc * TT:(tc + 1) * TT]
                        ).then_inc(sQv[b].h, 16)
                        sQv[b].n += 16
                        sync.dma_start(
                            qk3[:, b, kh, 1, :],
                            kre[:, kh, tc * TT:(tc + 1) * TT]
                        ).then_inc(sQv[b].h, 16)
                        sQv[b].n += 16

                def machload(c):
                    E.wait(sync, ("bw", c))
                    if c >= 1:
                        E.wait(sync, ("eb2", c - 1, NG4 - 1))
                        E.wait(sync, ("nbm", c - 1, NG4 - 1))
                    E.inc(sync.dma_start(
                        b_row[c % 2][:],
                        bscr[c % 2:c % 2 + 1, :]), ("brow", c))
                    E.inc(sync.dma_start(
                        b_grp[c % 2][:].rearrange("p (g i) -> p g i", g=8),
                        bscr[c % 2:c % 2 + 1, :].rearrange(
                            "p (g q i) -> (p q) g i", g=8, q=4)), ("bgrp", c))
                    E.inc(sync.dma_start(
                        bt_grp[c % 2][:],
                        p.d["beta_T"].rearrange("(g p) t -> p g t", p=4)[
                            :, :, c * 128:(c + 1) * 128]), ("btg", c))

                def vload(tc):
                    if tc >= 1:
                        E.wait(sync, ("sbf", tc * 4 - 1, NG4 - 1))
                    for vh in range(32):
                        sync.dma_start(
                            vb3[:, vh, :],
                            vre[:, vh, tc * TT:(tc + 1) * TT]
                        ).then_inc(sVv.h, 16)
                        sVv.n += 16

                qvload(0)
                qvload(1)
                vload(0)
                machload(0)
                machload(1)
                for c in range(NCH):
                    if c % 4 == 2 and (c // 4 + 2) < NT:
                        qvload(c // 4 + 2)
                    if c % 4 == 3 and (c // 4 + 1) < NT:
                        vload(c // 4 + 1)
                    if c + 2 < NCH:
                        machload(c + 2)

            @blk.gpsimd
            def _(g):
                def machw(c):
                    E.wait(g, ("bsb", c))
                    if c >= 2:
                        E.wait(g, ("brow", c - 2))
                        E.wait(g, ("bgrp", c - 2))
                    E.inc(g.dma_start(
                        bscr[c % 2].rearrange("(h i) -> h i", h=32),
                        b_sb[c % 2][:]), ("bw", c))

                machw(0)
                machw(1)
                for c in range(NCH):
                    g.wait_ge(sQv[(c // 4) % 2].h, 16 * 32 * (c // 8 + 1))
                    for gq in range(NG4):
                        for i4 in range(4):
                            h = gq * 4 + i4
                            E.wait(g, ("eb", c, gq))
                            k = ("ws", c, gq - 1, 3) if gq else \
                                (("ws", c - 1, NG4 - 1, 3) if c else None)
                            wif(g, k)
                            E.inc(g.tensor_mul(s4(wfm_w, i4), KF(c, h),
                                               s4(EBt, i4)), ("wfm", c, gq, i4))
                        for i4 in range(4):
                            h = gq * 4 + i4
                            k = ("omm", c, gq - 1, 3) if gq else \
                                (("omm", c - 1, NG4 - 1, 3) if c else None)
                            wif(g, k)
                            E.inc(g.tensor_mul(s4(qt_w, i4), QF(c, h),
                                               s4(EBt, i4)), ("qtm", c, gq, i4))
                        for i4 in range(4):
                            h = gq * 4 + i4
                            E.wait(g, ("gup", c, gq, NLV))
                            k = ("smm", c, gq - 1, 3) if gq else \
                                (("smm", c - 1, NG4 - 1, 3) if c else None)
                            wif(g, k)
                            E.inc(g.tensor_scalar_mul(
                                s4(dlt_w, i4), s4(gam, i4),
                                beta_sb[:, c * 32 + h:c * 32 + h + 1]),
                                ("dlt", c, gq, i4))
                        for i4 in range(4):
                            h = gq * 4 + i4
                            E.wait(g, ("eb2", c, gq))
                            k = ("ktr", c, gq - 1) if gq else \
                                (("ktr", c - 1, NG4 - 1) if c else None)
                            wif(g, k)
                            E.inc(g.tensor_mul(s4(ktl_w, i4), KF(c, h),
                                               s4(EB2t, i4)), ("ktl", c, gq, i4))
                        E.wait(g, ("oev", c, gq))
                        for i4 in range(4):
                            h = gq * 4 + i4
                            E.inc(g.dma_start(
                                p.d["oa"][h * 128:(h + 1) * 128,
                                          c * 128:(c + 1) * 128],
                                s4(ostg[:, (gq % 2) * 512:
                                        (gq % 2 + 1) * 512], i4)),
                                ("ost", c, gq, i4))
                    if c + 2 < NCH:
                        machw(c + 2)
                for nm in ("o00", "o01", "o10", "o11"):
                    g.wait_ge(E.sems[nm].h, E.cnt[nm])

            @blk.tensor
            def _(t):
                t.wait_ge(E.sems["c"].h, E.cnt["c"])

                def macht(c):
                    if c == 1:
                        E.wait(t, ("ebcv", 0))
                    elif c >= 2:
                        E.wait(t, ("eb2", c - 2, NG4 - 1))
                    k2 = ("bw", c - 2) if c >= 2 else None
                    wif(t, k2)
                    wif(t, ("ebc1", c - 2) if c >= 2 else None)
                    E.inc(t.matmul(ps_misc[0:32, 0:128],
                                   g_sb[:, c * 32:(c + 1) * 32],
                                   U128[:], start=True, stop=True), ("bmm", c))
                    E.wait(t, ("ebc1", c))
                    wif(t, ("ebrow", c - 2) if c >= 2 else None)
                    E.inc(t.transpose(ps_misc[0:1, 128:160],
                                      ebc32[c % 2][:],
                                      idf[0:32, 0:32]), ("ebtr", c))
                    E.wait(t, ("ebrow", c))
                    wif(t, ("ebcv", c - 2) if c >= 2 else None)
                    E.inc(t.matmul(ps_misc[:, 160:192], ones_r[:],
                                   ebcrow[c % 2][:], start=True, stop=True),
                          ("ebbc", c))

                macht(0)
                macht(1)
                for c in range(NCH):
                    tc = c // 4
                    for gq in range(NG4):
                        heads = [gq * 4 + i for i in range(4)]
                        if gq == 0:
                            E.wait(t, ("ebcv", min(c + 1, NCH - 1)))
                        else:
                            E.wait(t, ("eb", c, gq - 1)) if False else None
                            wif(t, ("eb2", c, gq - 1))
                        E.wait(t, ("brow", c))
                        E.inc(t.matmul(
                            ps_misc[:], ones_r[:],
                            b_row[c % 2][0:1, gq * 512:(gq + 1) * 512],
                            start=True, stop=True), ("t1", c, gq))
                        wif(t, pg(c, gq) and ("gup",) + pg(c, gq) + (NLV,))
                        E.wait(t, ("bgrp", c))
                        t.matmul(ps_r2[:], ones_r[:],
                                 b_row[c % 2][0:1, gq * 512:(gq + 1) * 512],
                                 start=True, stop=False)
                        E.inc(t.matmul(ps_r2[:],
                                       b_grp[c % 2][:, gq * 128:(gq + 1) * 128],
                                       nsel4[:], start=False, stop=True),
                              ("r2", c, gq))
                        wif(t, pg(c, gq) and ("n1a",) + pg(c, gq))
                        E.wait(t, ("btg", c))
                        E.inc(t.matmul(ps_kk[:],
                                       bt_grp[c % 2][:, gq * 128:(gq + 1) * 128],
                                       nsel4[:], start=True, stop=True),
                              ("nbm", c, gq))
                        E.wait(t, ("eb", c, gq))
                        E.inc(t.matmul(ps_misc[0:1, 0:512],
                                       b_grp[c % 2][:, gq * 128 + 127:
                                                    gq * 128 + 128],
                                       sel4[:], start=True, stop=True),
                              ("bcr", c, gq))
                        E.wait(t, ("row3", c, gq))
                        E.inc(t.matmul(ps_misc[:], ones_r[:], row3[:],
                                       start=True, stop=True), ("eb2m", c, gq))
                        E.wait(t, ("negb", c, gq))
                        sQ = sQv[tc % 2]
                        t.wait_ge(sQ.h, 16 * 32 * (tc // 2 + 1))
                        for i4, h in enumerate(heads):
                            E.inc(t.matmul(s4(ps_kk, i4), KF(c, h), KF(c, h),
                                           start=True, stop=True),
                                  ("kk", c, gq, i4))
                        wif(t, pg(c, gq) and ("mtv",) + pg(c, gq))
                        for i4, h in enumerate(heads):
                            E.inc(t.matmul(s4(ps_qk, i4), KF(c, h), QF(c, h),
                                           start=True, stop=True),
                                  ("qk", c, gq, i4))
                        wif(t, pg(c, gq) and ("sup",) + pg(c, gq) + (3,))
                        if c % SEGC == 0:
                            E.wait(t, ("sz", c))
                        else:
                            E.wait(t, ("sbf", c - 1, gq))
                        for i4, h in enumerate(heads):
                            E.wait(t, ("wfm", c, gq, i4))
                            E.inc(t.matmul(s4(ps_ws, i4),
                                           S_bf[:, h * 128:(h + 1) * 128],
                                           s4(wfm_w, i4),
                                           start=True, stop=True),
                                  ("ws", c, gq, i4))
                        wif(t, pg(c, gq) and ("ktm",) + pg(c, gq))
                        for i4 in range(4):
                            E.wait(t, ("ufm", c, gq, i4))
                            ins = t.transpose(s4(ps_tq, i4),
                                              s4(u_fm_w, i4), idb[:])
                        E.inc(ins, ("utr", c, gq))
                        E.wait(t, ("utm", c, gq))
                        E.wait(t, ("n1b", c, gq))
                        for i4 in range(4):
                            ins = t.transpose(s4(ps_tq, i4), s4(N1t, i4),
                                              idb[:])
                        E.inc(ins, ("x1tr", c, gq))
                        E.wait(t, ("mskv", c, gq))
                        for i4 in range(4):
                            E.inc(t.matmul(s4(ps_r2, i4), s4(N1t, i4),
                                           s4(u_tm_w, i4),
                                           start=True, stop=True),
                                  ("g0m", c, gq, i4))
                        for lvl in range(1, NLV + 1):
                            if lvl == 1:
                                E.wait(t, ("xu0", c, gq))
                                srcT, srcU = N1t, Xu0
                            else:
                                E.wait(t, ("xte", c, gq, lvl - 1))
                                E.wait(t, ("xue", c, gq, lvl - 1))
                                par = (lvl - 1) % 2
                                srcT = Xt_sb[:, par * 512:(par + 1) * 512]
                                srcU = Xu_sb[:, par * 512:(par + 1) * 512]
                            if lvl >= 2:
                                pass
                            if lvl == 1:
                                k = pg(c, gq)
                                wif(t, k and ("xte",) + k + (NLV,))
                                k2 = pg(c, gq)
                                wif(t, k2 and ("xue",) + k2 + (NLV - 1,))
                            for i4 in range(4):
                                ins = t.matmul(
                                    s4(ps_s1, i4),
                                    srcU[:, i4 * 128:(i4 + 1) * 128],
                                    srcT[:, i4 * 128:(i4 + 1) * 128],
                                    start=True, stop=True)
                            if lvl < NLV:
                                for i4 in range(4):
                                    ins = t.matmul(
                                        s4(ps_s2, i4),
                                        srcT[:, i4 * 128:(i4 + 1) * 128],
                                        srcU[:, i4 * 128:(i4 + 1) * 128],
                                        start=True, stop=True)
                            E.inc(ins, ("sql", c, gq, lvl))
                            E.wait(t, ("xte", c, gq, lvl))
                            if lvl == 1:
                                E.wait(t, ("g0", c, gq))
                            else:
                                E.wait(t, ("gup", c, gq, lvl - 1))
                            par = lvl % 2
                            for i4 in range(4):
                                E.inc(t.matmul(
                                    s4(ps_r2, i4),
                                    Xt_sb[:, par * 512 + i4 * 128:
                                          par * 512 + i4 * 128 + 128],
                                    s4(gam, i4), start=True, stop=True),
                                    ("gmm", c, gq, lvl, i4))
                        E.wait(t, ("mtv", c, gq))
                        E.wait(t, ("gup", c, gq, NLV))
                        for i4, h in enumerate(heads):
                            E.wait(t, ("qtm", c, gq, i4))
                            E.wait(t, ("dlt", c, gq, i4))
                            t.matmul(s4(ps_ws, i4),
                                     S_bf[:, h * 128:(h + 1) * 128],
                                     s4(qt_w, i4), start=True, stop=False)
                            E.inc(t.matmul(s4(ps_ws, i4), s4(dlt_w, i4),
                                           s4(Mt, i4), start=False, stop=True),
                                  ("omm", c, gq, i4))
                        E.wait(t, ("xu0", c, gq))
                        for i4 in range(4):
                            E.wait(t, ("ktl", c, gq, i4))
                            ins = t.transpose(s4(ps_tq, i4), s4(ktl_w, i4),
                                              idb[:])
                        E.inc(ins, ("ktr", c, gq))
                        E.wait(t, ("oev", c, gq))
                        E.wait(t, ("ktm", c, gq))
                        for i4 in range(4):
                            E.inc(t.matmul(s4(ps_ws, i4), s4(ktm_w, i4),
                                           s4(dlt_w, i4),
                                           start=True, stop=True),
                                  ("smm", c, gq, i4))
                    if c + 2 < NCH:
                        macht(c + 2)

            @blk.vector
            def _(v):
                v.wait_ge(E.sems["c"].h, E.cnt["c"])

                def machv(c):
                    E.wait(v, ("bmm", c))
                    wif(v, ("bw", c - 2) if c >= 2 else None)
                    wif(v, ("ebc1", c - 2) if c >= 2 else None)
                    E.inc(v.tensor_copy(b_sb[c % 2][:], ps_misc[0:32, 0:128]),
                          ("bsb", c))
                    E.wait(v, ("ebbc", c))
                    if c >= 2:
                        E.wait(v, ("sup", c - 2, NG4 - 1, 3))
                    E.inc(v.tensor_copy(ebC_bc[c % 2][:],
                                        ps_misc[:, 160:192]), ("ebcv", c))

                machv(0)
                machv(1)
                for c in range(NCH):
                    v.wait_ge(sVv.h, 16 * 32 * (c // 4 + 1))
                    if c % SEGC == 0:
                        if c > 0:
                            E.wait(v, ("sbf", c - 1, NG4 - 1))
                        v.memset(S_sb[:], 0.0)
                        E.inc(v.memset(S_bf[:], 0.0), ("sz", c))
                    for gq in range(NG4):
                        E.wait(v, ("r2", c, gq))
                        k = pg(c, gq)
                        wif(v, k and ("estr",) + k)
                        E.inc(v.tensor_add(mtmp[:], ps_r2[:], m4[:]),
                              ("mskv", c, gq))
                        E.wait(v, ("nbm", c, gq))
                        wif(v, k and ("n1b",) + k)
                        E.inc(v.tensor_copy(negb[:], ps_kk[:]),
                              ("negb", c, gq))
                        E.wait(v, ("estr", c, gq))
                        wif(v, k and ("mtv",) + k)
                        E.inc(v.tensor_add(Einc[:], Estr[:], i4c[:]),
                              ("einc", c, gq))
                        E.wait(v, ("bcr", c, gq))
                        wif(v, k and ("eb2m",) + k)
                        E.inc(v.tensor_sub(
                            row3[:], ps_misc[0:1, 0:512],
                            b_row[c % 2][0:1, gq * 512:(gq + 1) * 512]),
                            ("row3", c, gq))
                        E.wait(v, ("kk", c, gq, 3))
                        E.inc(v.tensor_mul(ntmp[:], ps_kk[:], Estr[:]),
                              ("n1a", c, gq))
                        E.wait(v, ("n1a", c, gq))
                        k2 = pg(c, gq)
                        wif(v, k2 and ("sql",) + k2 + (1,))
                        E.inc(v.tensor_mul(N1t[:], ntmp[:], negb[:]),
                              ("n1b", c, gq))
                        E.wait(v, ("qk", c, gq, 3))
                        E.wait(v, ("einc", c, gq))
                        wif(v, k2 and ("omm",) + k2 + (3,))
                        E.inc(v.tensor_mul(Mt[:], ps_qk[:], Einc[:]),
                              ("mtv", c, gq))
                        for i4 in range(4):
                            h = gq * 4 + i4
                            E.wait(v, ("ws", c, gq, i4))
                            if i4 == 0:
                                wif(v, k2 and ("utr",) + k2)
                            E.inc(v.tensor_sub(s4(u_fm_w, i4), VF(c, h),
                                               s4(ps_ws, i4)),
                                  ("ufm", c, gq, i4))
                        E.wait(v, ("g0m", c, gq, 3))
                        E.wait(v, ("utm", c, gq))
                        wif(v, k2 and ("dlt",) + k2 + (3,))
                        E.inc(v.tensor_add(gam[:], u_tm_w[:], ps_r2[:]),
                              ("g0", c, gq))
                        for lvl in range(1, NLV + 1):
                            E.wait(v, ("gmm", c, gq, lvl, 3))
                            E.inc(v.tensor_add(gam[:], gam[:], ps_r2[:]),
                                  ("gup", c, gq, lvl))
                        for i4 in range(4):
                            h = gq * 4 + i4
                            E.wait(v, ("smm", c, gq, i4))
                            if i4 == 0:
                                E.wait(v, ("ebcv", c))
                                if c % SEGC != 0:
                                    E.wait(v, ("sbf", c - 1, gq))
                            E.inc(v.scalar_tensor_tensor(
                                S_sb[:, h * 128:(h + 1) * 128],
                                S_sb[:, h * 128:(h + 1) * 128],
                                ebC_bc[c % 2][:, h:h + 1],
                                s4(ps_ws, i4), ALU.mult, ALU.add),
                                ("sup", c, gq, i4))
                    if c + 2 < NCH:
                        machv(c + 2)

            @blk.scalar
            def _(s):
                def machs(c):
                    E.wait(s, ("bsb", c))
                    wif(s, ("ebtr", c - 2) if c >= 2 else None)
                    E.inc(s.activation(ebc32[c % 2][:],
                                       b_sb[c % 2][:, 127:128], AF.Exp),
                          ("ebc1", c))
                    E.wait(s, ("ebtr", c))
                    wif(s, ("ebbc", c - 2) if c >= 2 else None)
                    E.inc(s.copy(ebcrow[c % 2][:], ps_misc[0:1, 128:160]),
                          ("ebrow", c))

                machs(0)
                machs(1)
                for c in range(NCH):
                    for gq in range(NG4):
                        k = pg(c, gq)
                        E.wait(s, ("t1", c, gq))
                        wif(s, k and ("wfm",) + k + (3,))
                        wif(s, k and ("qtm",) + k + (3,))
                        E.inc(s.activation(EBt[:], ps_misc[:], AF.Exp),
                              ("eb", c, gq))
                        E.wait(s, ("mskv", c, gq))
                        wif(s, k and ("n1a",) + k)
                        E.inc(s.activation(Estr[:], mtmp[:], AF.Exp),
                              ("estr", c, gq))
                        E.wait(s, ("eb2m", c, gq))
                        wif(s, k and ("ktl",) + k + (3,))
                        E.inc(s.activation(EB2t[:], ps_misc[:], AF.Exp),
                              ("eb2", c, gq))
                        E.wait(s, ("utr", c, gq))
                        wif(s, k and ("g0m",) + k + (3,) if False else None)
                        wif(s, k and ("g0",) + k)
                        E.inc(s.copy(u_tm_w[:], ps_tq[:]), ("utm", c, gq))
                        E.wait(s, ("x1tr", c, gq))
                        wif(s, k and ("sql",) + k + (1,))
                        E.inc(s.copy(Xu0[:], ps_tq[:]), ("xu0", c, gq))
                        for lvl in range(1, NLV + 1):
                            E.wait(s, ("sql", c, gq, lvl))
                            par = lvl % 2
                            if lvl >= 3:
                                E.wait(s, ("gmm", c, gq, lvl - 2, 3))
                            elif lvl <= 2:
                                kk2 = pg(c, gq)
                                wif(s, kk2 and ("gmm",) + kk2
                                    + (NLV - 2 + lvl, 3))
                            E.inc(s.copy(
                                Xt_sb[:, par * 512:(par + 1) * 512],
                                ps_s1[:]), ("xte", c, gq, lvl))
                            if lvl < NLV:
                                if lvl >= 2:
                                    E.wait(s, ("sql", c, gq, lvl + 1)) \
                                        if False else None
                                E.inc(s.copy(
                                    Xu_sb[:, par * 512:(par + 1) * 512],
                                    ps_s2[:]), ("xue", c, gq, lvl))
                        E.wait(s, ("omm", c, gq, 3))
                        k2o = (c, gq - 2) if gq >= 2 else                             ((c - 1, gq + 6) if c >= 1 else None)
                        if k2o is not None:
                            wif(s, ("ost",) + k2o + (2,))
                            wif(s, ("ost",) + k2o + (3,))
                        E.inc(s.copy(ostg[:, (gq % 2) * 512:(gq % 2 + 1) * 512],
                                     ps_ws[:]), ("oev", c, gq))
                        E.wait(s, ("ktr", c, gq))
                        wif(s, k and ("smm",) + k + (3,))
                        E.inc(s.copy(ktm_w[:], ps_tq[:]), ("ktm", c, gq))
                        E.wait(s, ("sup", c, gq, 3))
                        wif(s, k and ("ws",) + k + (3,) if False else None)
                        E.inc(s.copy(S_bf[:, gq * 512:(gq + 1) * 512],
                                     S_sb[:, gq * 512:(gq + 1) * 512]),
                              ("sbf", c, gq))
                    if c + 2 < NCH:
                        machs(c + 2)


def emit_gate_out(p):
    nc = p.nc
    E = EvSys(nc, "p3")
    for i in range(3):
        E.define(("c", i), "c", 16)
    E.define(("cg", 0), "cg", 16)
    E.define(("wout",), "wl0", 16)
    for tt in range(NT):
        for vh in range(HV):
            E.define(("lo", tt, vh), f"lo{vh % 2}", 16)
            E.define(("lz", tt, vh), f"lz{vh % 2}", 16)
            E.define(("zs", tt, vh), "s1", 1)
            E.define(("y0", tt, vh), "v1", 1)
            E.define(("sq", tt, vh), "s2", 1)
            E.define(("ssm", tt, vh), "t1", 1)
            E.define(("rst", tt, vh), "s3", 1)
            E.define(("bcm", tt, vh), "t2", 1)
            E.define(("yv", tt, vh), "v2", 1)
        for tm in range(4):
            for cn in range(4):
                E.define(("mm", tt, tm, cn), "t", 1)
                E.define(("ev", tt, tm, cn), "sv", 1)
            E.define(("st", tt, tm), f"vo{(tt * 4 + tm) % 2}", 16)

    import contextlib
    with contextlib.ExitStack() as _st:
        en = _st.enter_context
        ocb = en(nc.sbuf_tensor("p3_ocb", [128, 1], BF16))
        onr = en(nc.sbuf_tensor("p3_onr", [1, 128], BF16))
        nwc = en(nc.sbuf_tensor("p3_nwc", [128, 1], F32))
        r1e = en(nc.sbuf_tensor("p3_r1e", [1, 1], F32))
        wout = en(nc.sbuf_tensor("p3_wout", [128, 32 * 2048], BF16))
        ot = en(nc.sbuf_tensor("p3_ot", [128, 2 * 512], BF16))
        zt = en(nc.sbuf_tensor("p3_zt", [128, 2 * 512], BF16))
        zs_t = en(nc.sbuf_tensor("p3_zs", [128, 512], BF16))
        y0_t = en(nc.sbuf_tensor("p3_y0", [128, 512], F32))
        sq_t = en(nc.sbuf_tensor("p3_sq", [128, 512], BF16))
        lnr = en(nc.sbuf_tensor("p3_lnr", [1, 512], F32))
        rsr = en(nc.sbuf_tensor("p3_rsr", [1, 512], BF16))
        yw = en(nc.sbuf_tensor("p3_yw", [128, 32 * 512], BF16))
        vstage = en(nc.sbuf_tensor("p3_vst", [128, 2 * D], F32))
        psr = en(nc.psum_tensor("p3_psr", [1, 512], F32))
        psb = en(nc.psum_tensor("p3_psb", [128, 512], F32))
        psm = [en(nc.psum_tensor(f"p3_ps{i}", [128, 512], F32))
               for i in range(2)]
        sFs = Sem(nc, "p3fs")

        ore = p.d["oa"].rearrange("(vh q) t -> q vh t", q=128)
        zre = p.d["za"].rearrange("(vh q) t -> q vh t", q=128)
        wre = p.d["Wout"].rearrange("(kc q) c -> q kc c", q=128)
        w3 = wout[:].rearrange("q (kc c) -> q kc c", kc=32)

        with nc.Block() as blk:
            @blk.sync
            def _(sync):
                E.inc(sync.dma_start(ocb[:], p.d["c_ones_col_bf"][:]), ("c", 0))
                E.inc(sync.dma_start(onr[:], p.d["c_ones_1x128_bf"][:]), ("c", 1))
                E.inc(sync.dma_start(r1e[:], p.d["c_row1_eps6"][:]), ("c", 2))
                for tt in range(NT):
                    for vh in range(HV):
                        u = tt * HV + vh
                        if u >= 2:
                            pu = u - 2
                            E.wait(sync, ("yv", pu // HV, pu % HV))
                        E.inc(sync.dma_start(
                            ot[:, (vh % 2) * 512:(vh % 2 + 1) * 512],
                            ore[:, vh, tt * TT:(tt + 1) * TT]), ("lo", tt, vh))
                        E.inc(sync.dma_start(
                            zt[:, (vh % 2) * 512:(vh % 2 + 1) * 512],
                            zre[:, vh, tt * TT:(tt + 1) * TT]), ("lz", tt, vh))

            @blk.gpsimd
            def _(g):
                E.inc(g.dma_start(nwc[:], p.d["norm_w"][:]), ("cg", 0))
                E.inc(g.dma_start(w3, wre), ("wout",))
                for tt in range(NT):
                    for tm in range(4):
                        E.wait(g, ("ev", tt, tm, 3))
                        sl = (tt * 4 + tm) % 2
                        E.inc(g.dma_start(
                            p.d["sub_out"][(tt * 4 + tm) * 128:
                                           (tt * 4 + tm) * 128 + 128, :],
                            vstage[:, sl * D:(sl + 1) * D]), ("st", tt, tm))
                g.wait_ge(E.sems["vo0"].h, E.cnt["vo0"])
                g.wait_ge(E.sems["vo1"].h, E.cnt["vo1"])

            @blk.scalar
            def _(s):
                s.wait_ge(E.sems["c"].h, E.cnt["c"])
                s.wait_ge(E.sems["cg"].h, E.cnt["cg"])
                for tt in range(NT):
                    for vh in range(HV):
                        u = tt * HV + vh
                        E.wait(s, ("lz", tt, vh))
                        if u >= 1:
                            pu = u - 1
                            E.wait(s, ("y0", pu // HV, pu % HV))
                        E.inc(s.activation(
                            zs_t[:], zt[:, (vh % 2) * 512:(vh % 2 + 1) * 512],
                            AF.Silu), ("zs", tt, vh))
                        E.wait(s, ("y0", tt, vh))
                        if u >= 1:
                            pu = u - 1
                            E.wait(s, ("ssm", pu // HV, pu % HV))
                        E.inc(s.activation(sq_t[:], y0_t[:], AF.Square),
                              ("sq", tt, vh))
                        E.wait(s, ("ssm", tt, vh))
                        ins = s.activation(lnr[:], psr[0:1, :], AF.Ln,
                                           bias=r1e[:], scale=1.0 / 128)
                        fence(s, sFs, ins)
                        if u >= 1:
                            pu = u - 1
                            E.wait(s, ("bcm", pu // HV, pu % HV))
                        E.inc(s.activation(rsr[:], lnr[:], AF.Exp,
                                           scale=-0.5), ("rst", tt, vh))

            @blk.vector
            def _(v):
                v.wait_ge(E.sems["c"].h, E.cnt["c"])
                v.wait_ge(E.sems["cg"].h, E.cnt["cg"])
                for tt in range(NT):
                    for vh in range(HV):
                        u = tt * HV + vh
                        E.wait(v, ("lo", tt, vh))
                        E.wait(v, ("zs", tt, vh))
                        if u >= 1:
                            pu = u - 1
                            E.wait(v, ("sq", pu // HV, pu % HV))
                            E.wait(v, ("yv", pu // HV, pu % HV))
                        E.inc(v.tensor_mul(
                            y0_t[:], ot[:, (vh % 2) * 512:(vh % 2 + 1) * 512],
                            zs_t[:]), ("y0", tt, vh))
                        E.wait(v, ("bcm", tt, vh))
                        if tt >= 1 and vh == 0:
                            E.wait(v, ("mm", tt - 1, 3, 3))
                        E.inc(v.scalar_tensor_tensor(
                            yw[:, vh * 512:(vh + 1) * 512], y0_t[:], nwc[:],
                            psb[:], ALU.mult, ALU.mult), ("yv", tt, vh))
                    for tm in range(4):
                        sl = (tt * 4 + tm) % 2
                        for cn in range(4):
                            E.wait(v, ("mm", tt, tm, cn))
                            if cn == 0 and tt * 4 + tm >= 2:
                                pst_ = tt * 4 + tm - 2
                                E.wait(v, ("st", pst_ // 4, pst_ % 4))
                            E.inc(v.tensor_copy(
                                vstage[:, sl * D + cn * 512:
                                       sl * D + cn * 512 + 512],
                                psm[((tt * 4 + tm) * 4 + cn) % 2][:]),
                                ("ev", tt, tm, cn))

            @blk.tensor
            def _(t):
                t.wait_ge(E.sems["c"].h, E.cnt["c"])
                E.wait(t, ("wout",))
                for tt in range(NT):
                    for vh in range(HV):
                        u = tt * HV + vh
                        E.wait(t, ("sq", tt, vh))
                        if u >= 1:
                            pu = u - 1
                            E.wait(t, ("rst", pu // HV, pu % HV))
                        E.inc(t.matmul(psr[:], ocb[:], sq_t[:],
                                       start=True, stop=True), ("ssm", tt, vh))
                        E.wait(t, ("rst", tt, vh))
                        if u >= 1:
                            pu = u - 1
                            E.wait(t, ("yv", pu // HV, pu % HV))
                        E.inc(t.matmul(psb[:], onr[:], rsr[:],
                                       start=True, stop=True), ("bcm", tt, vh))
                    E.wait(t, ("yv", tt, HV - 1))
                    for tm in range(4):
                        for cn in range(4):
                            gidx = (tt * 4 + tm) * 4 + cn
                            if gidx >= 2:
                                pgx = gidx - 2
                                E.wait(t, ("ev", pgx // 16,
                                           (pgx % 16) // 4, pgx % 4))
                            for kc in range(32):
                                ins = t.matmul(
                                    psm[gidx % 2][:],
                                    yw[:, kc * 512 + tm * 128:
                                       kc * 512 + tm * 128 + 128],
                                    w3[:, kc, cn * 512:(cn + 1) * 512],
                                    start=(kc == 0), stop=(kc == 31))
                            E.inc(ins, ("mm", tt, tm, cn))



WEIGHT_SPECS = [
    ("Wqkv", [D, 8192], F32),
    ("Wz", [D, 4096], F32),
    ("Wab", [D, 64], F32),
    ("conv_w_rs", [128, 256], F32),
    ("A_log", [1, 32], F32),
    ("dt_bias", [1, 32], F32),
    ("Wq_p", [D, D], F32),
    ("Wk_p", [D, D], F32),
    ("Wv", [D, D], F32),
    ("Wo", [D, D], F32),
    ("Wout", [4096, D], F32),
    ("norm_w", [128, 1], F32),
    ("gate_w", [8192, D], I32),
    ("up_w", [8192, D], I32),
    ("down_w", [D, 8192], I32),
    ("gate_scale_rs", [128, 64], F32),
    ("up_scale_rs", [128, 64], F32),
    ("down_scale_rs", [128, 16], F32),
]

ALL_PHASES = ("P0", "P1", "P2", "P3", "P3b", "P4a", "P4b", "P4c",
              "P5", "P6", "P7", "P7b", "P8a", "P8b", "P8c")


def build_program(dev=False, phases=ALL_PHASES, dev_feed=()):
    nc = bass.Bass("TRN2", target_bir_lowering=False)
    p = Prog(nc, dev)
    p.d["x"] = nc.dram_tensor("x", [L, D], F32, kind="ExternalInput").ap()
    for nm, shape, dt in CONST_SPECS:
        p.d[nm] = nc.dram_tensor(nm, shape, dt, kind="ExternalInput").ap()
    for nm, shape, dt in WEIGHT_SPECS:
        p.d[nm] = nc.dram_tensor(nm, shape, dt, kind="ExternalInput").ap()

    def dram(name, shape, dtype):
        if name in dev_feed:
            ap = nc.dram_tensor(name, shape, dtype, kind="ExternalInput").ap()
            p.d[name] = ap
            return ap
        return p.dram(name, shape, dtype)

    dram("xn1", [D, L], BF16)
    dram("qa", [HK * 128, L], BF16)
    dram("ka", [HK * 128, L], BF16)
    dram("va", [HV * 128, L], BF16)
    dram("za", [HV * 128, L], BF16)
    dram("beta_T", [32, L], F32)
    dram("bscr", [2, 4096], F32)
    dram("oa", [HV * 128, L], BF16)
    dram("sub_out", [L, D], F32)
    dram("x2", [L, D], F32)
    dram("xn2", [D, L], BF16)
    dram("x3", [L, D], F32)
    dram("xn3", [D, L], BF16)
    dram("x4", [L, D], F32)
    dram("xn4", [D, L], BF16)
    dram("hbuf", [64 * 128, L], BF16)
    dram("qb", [HK * 128, L], BF16)
    dram("kb", [HK * 128, L], BF16)
    dram("vb", [L, D], BF16)
    dram("ob", [HK * 128, L], BF16)
    p.d["out"] = nc.dram_tensor("out", [L, D], F32, kind="ExternalOutput").ap()
    p.sb["g_sb"] = nc.alloc_sbuf_tensor("g_sb", [128, NCH * 32], F32)
    p.sb["beta_sb"] = nc.alloc_sbuf_tensor("beta_sb", [128, NCH * 32], F32)

    emitted = [False]

    def phase(name, fn):
        if name not in phases:
            return
        if emitted[0]:
            nc.all_engine_barrier()
        snap = nc.snapshot_sems()
        fn()
        nc.all_engine_barrier()
        nc.clear_and_free_semaphores(nc.allocated_since(snap))
        emitted[0] = True

    phase("P0", lambda: emit_ln_transpose(p, p.d["x"], p.d["xn1"], "p0"))
    phase("P1", lambda: emit_proj_a(p))
    phase("P2", lambda: emit_scan(p))
    phase("P3", lambda: emit_gate_out(p))
    phase("P3b", lambda: emit_ln_transpose(
        p, p.d["x"], p.d["xn2"], "p3b",
        add_ap=p.d["sub_out"], xout_ap=p.d["x2"]))
    phase("P4a", lambda: emit_ffn_a(p, "xn2", "f1a"))
    phase("P4b", lambda: emit_ffn_b(p, p.d["sub_out"], "f1b"))
    phase("P4c", lambda: emit_ln_transpose(
        p, p.d["x2"], p.d["xn3"], "p4c",
        add_ap=p.d["sub_out"], xout_ap=p.d["x3"]))
    phase("P5", lambda: emit_proj_b(p))
    phase("P6", lambda: emit_wattn(p))
    phase("P7", lambda: emit_wo(p))
    phase("P7b", lambda: emit_ln_transpose(
        p, p.d["x3"], p.d["xn4"], "p7b",
        add_ap=p.d["sub_out"], xout_ap=p.d["x4"]))
    phase("P8a", lambda: emit_ffn_a(p, "xn4", "f2a"))
    phase("P8b", lambda: emit_ffn_b(p, p.d["sub_out"], "f2b"))
    phase("P8c", lambda: emit_ln_transpose(
        p, p.d["x4"], None, "p8c",
        add_ap=p.d["sub_out"], xout_ap=p.d["out"]))

    if dev and "P1" in phases:
        nc.all_engine_barrier()
        dbg_g = p.dram("dbg_g", [128, NCH * 32], F32)
        dbg_b = p.dram("dbg_b", [128, NCH * 32], F32)
        sDbg = Sem(nc, "dbg")
        with nc.Block() as blk:
            @blk.gpsimd
            def _(g):
                g.dma_start(dbg_g, p.sb["g_sb"][:]).then_inc(sDbg.h, 16)
                g.dma_start(dbg_b, p.sb["beta_sb"][:]).then_inc(sDbg.h, 16)
                g.wait_ge(sDbg.h, 32)
    return nc, p


def host_inputs(I):
    cs = host_constants()
    m = {k: np.asarray(v) for k, v in cs.items()}
    m["Wqkv"] = np.asarray(I["L0_attn_a_Wqkv"], np.float32)
    m["Wz"] = np.asarray(I["L0_attn_a_Wz"], np.float32)
    m["Wab"] = np.concatenate([np.asarray(I["L0_attn_a_Wa"], np.float32),
                               np.asarray(I["L0_attn_a_Wb"], np.float32)], 1)
    cw = np.asarray(I["L0_attn_a_conv_w"], np.float32)
    m["conv_w_rs"] = np.ascontiguousarray(
        cw.reshape(64, 128, 4).transpose(1, 0, 2).reshape(128, 256))
    m["A_log"] = np.asarray(I["L0_attn_a_A_log"], np.float32).reshape(1, 32)
    m["dt_bias"] = np.asarray(I["L0_attn_a_dt_bias"], np.float32).reshape(1, 32)
    m["Wout"] = np.asarray(I["L0_attn_a_Wout"], np.float32)
    m["norm_w"] = np.asarray(I["L0_attn_a_norm_w"], np.float32).reshape(128, 1)
    m["Wq_p"] = permute_rope_weight(I["L0_attn_b_Wq"]).astype(np.float32)
    m["Wk_p"] = permute_rope_weight(I["L0_attn_b_Wk"]).astype(np.float32)
    m["Wv"] = np.asarray(I["L0_attn_b_Wv"], np.float32)
    m["Wo"] = np.asarray(I["L0_attn_b_Wo"], np.float32)
    m["gate_w"] = np.asarray(I["L0_ffn_gate_w"], np.int32)
    m["up_w"] = np.asarray(I["L0_ffn_up_w"], np.int32)
    m["down_w"] = np.asarray(I["L0_ffn_down_w"], np.int32)
    m["gate_scale_rs"] = np.ascontiguousarray(
        np.asarray(I["L0_ffn_gate_scale"], np.float32).reshape(64, 128).T)
    m["up_scale_rs"] = np.ascontiguousarray(
        np.asarray(I["L0_ffn_up_scale"], np.float32).reshape(64, 128).T)
    m["down_scale_rs"] = np.ascontiguousarray(
        np.asarray(I["L0_ffn_down_scale"], np.float32).reshape(16, 128).T)
    return m


def kernel(**inputs):
    _check_cu_seqlens(inputs["L0_attn_a_cu_seqlens"])
    nc, p = build_program(dev=False)
    base = host_inputs(inputs)
    xfull = np.ascontiguousarray(np.asarray(inputs["L0_attn_a_x"], np.float32))
    in_maps = []
    for r in range(8):
        m = dict(base)
        m["x"] = xfull[r]
        in_maps.append(m)
    res = run_bass_kernel_spmd(nc, in_maps, core_ids=list(range(8)))
    return np.stack([np.asarray(res.results[r]["out"]) for r in range(8)])



def _check_cu_seqlens(cu):
    expect = [0, 2048, 4096, 6144, 8192]
    got = [int(v) for v in np.asarray(cu).reshape(-1)]
    assert got == expect, f"unsupported cu_seqlens {got}"
